# version 15
# baseline (speedup 1.0000x reference)
import numpy as np
import ml_dtypes
from contextlib import ExitStack

import concourse.bass as bass
import concourse.mybir as mybir
from concourse.bass_utils import run_bass_kernel_spmd

F32 = mybir.dt.float32
BF16 = mybir.dt.bfloat16
ALU = mybir.AluOpType
ACT = mybir.ActivationFunctionType
AX = mybir.AxisListType

P = 128
DM = 2048
NSUB = DM // P
RPC = 1024
NRT = RPC // P
MBS = 512
NMB = DM // MBS
II = 8192
ICS = 512
NIC = II // ICS
NQ = ICS // P
OBS = 512
NOB = DM // OBS
ISUBS = II // P
NCORES = 8
EPS = 1e-5

ENGS = ("sync", "gpsimd", "scalar", "vector", "tensor")


class Sem:
    def __init__(self, h):
        self.h = h
        self.n = 0


class Slot:
    def __init__(self):
        self.free = None


class Ph:
    def __init__(self, nc):
        self.nc = nc
        self.q = {e: [] for e in ENGS}

    def emit(self, eng, fn, waits=(), inc=None, amt=1):
        ws = [(sm, val) for (sm, val) in waits if sm is not None and val > 0]
        self.q[eng].append((ws, fn, inc, amt))
        if inc is not None:
            inc.n += amt
            return inc.n
        return None

    def run(self):
        with self.nc.Block() as block:
            for e in ENGS:
                items = self.q[e]
                if not items:
                    continue

                def body(eng, items=items):
                    for ws, fn, inc, amt in items:
                        for sm, val in ws:
                            eng.wait_ge(sm.h, val)
                        inst = fn(eng)
                        if inc is not None:
                            inst.then_inc(inc.h, amt)

                getattr(block, e)(body)


class LNScratch:
    NSET = 4

    def __init__(self, nc, ctx):
        t = ctx.enter_context(nc.sbuf_tensor("ln_scr", [P, 8 * self.NSET], F32))
        self.ssum = [t[:, 8 * i + 0 : 8 * i + 1] for i in range(self.NSET)]
        self.ssq = [t[:, 8 * i + 1 : 8 * i + 2] for i in range(self.NSET)]
        self.ex2 = [t[:, 8 * i + 2 : 8 * i + 3] for i in range(self.NSET)]
        self.var = [t[:, 8 * i + 4 : 8 * i + 5] for i in range(self.NSET)]
        self.std = [t[:, 8 * i + 5 : 8 * i + 6] for i in range(self.NSET)]
        ds = [ctx.enter_context(nc.sbuf_tensor(f"ln_dump{i}", [P, DM], mybir.dt.float8e4))
              for i in range(2)]
        self.dumps = [x[:, :] for x in ds]
        self.i = 0
        self.last_ex2 = [0] * self.NSET
        self.last_mean = [0] * self.NSET
        self.last_sqrt = [0] * self.NSET


def ln_sums(ph, st, sems, x_ap, waits_v=(), waits_s=()):
    v, s = sems["v"], sems["s"]
    i = st.i % st.NSET
    dump = st.dumps[st.i % 2]
    dump_b = st.dumps[(st.i + 1) % 2]
    st.i += 1
    ph.emit(
        "scalar",
        lambda e: e.activation(dump, x_ap, ACT.Square, accum_out=st.ssq[i]),
        waits=list(waits_s) + [(v, st.last_ex2[i])],
    )
    ph.emit(
        "scalar",
        lambda e: e.activation(dump_b, x_ap, ACT.Copy, accum_out=st.ssum[i]),
        waits=list(waits_v) + [(v, st.last_mean[i])],
    )
    sm = ph.emit("scalar", lambda e: e.drain(), inc=s)
    return (i, sm)


def ln_fin(ph, st, sems, eps_ap, token, mean_ap, inv_ap):
    v, s = sems["v"], sems["s"]
    i, sm = token
    ssum, ssq, ex2, var, std = st.ssum[i], st.ssq[i], st.ex2[i], st.var[i], st.std[i]
    mn = ph.emit(
        "vector",
        lambda e: e.tensor_scalar(mean_ap, ssum, 1.0 / DM, None, ALU.mult, ALU.bypass),
        waits=[(s, sm)],
        inc=v,
    )
    st.last_mean[i] = mn
    ex = ph.emit(
        "vector",
        lambda e: e.tensor_scalar(ex2, ssq, 1.0 / DM, None, ALU.mult, ALU.bypass),
        inc=v,
    )
    st.last_ex2[i] = ex
    ph.emit("vector", lambda e: None, waits=[(v, ex)])
    nv = ph.emit(
        "vector",
        lambda e: e.scalar_tensor_tensor(var, mean_ap, mean_ap, ex2,
                                         ALU.mult, ALU.subtract),
        waits=[(s, st.last_sqrt[i])],
        inc=v,
    )
    sq2 = ph.emit(
        "scalar",
        lambda e: e.activation(std, var, ACT.Sqrt, scale=-1.0, bias=eps_ap),
        waits=[(v, nv)],
        inc=s,
    )
    st.last_sqrt[i] = sq2
    rv = ph.emit("vector", lambda e: e.reciprocal(inv_ap, std), waits=[(s, sq2)], inc=v)
    ph.emit("vector", lambda e: None, waits=[(v, rv)])
    return rv, sm


def ln_stats(ph, st, sems, eps_ap, x_ap, mean_ap, inv_ap, waits_v=(), waits_s=()):
    tok = ln_sums(ph, st, sems, x_ap, waits_v=waits_v, waits_s=waits_s)
    return ln_fin(ph, st, sems, eps_ap, tok, mean_ap, inv_ap)


def build_nc():
    nc = bass.Bass("TRN2", target_bir_lowering=False)

    xrows_h = nc.dram_tensor("xrows", [RPC, DM], F32, kind="ExternalInput").ap()
    xrep_h = nc.dram_tensor("xrep", [256, DM], F32, kind="ExternalInput").ap()
    blt_h = nc.dram_tensor("blt", [256, 256], F32, kind="ExternalInput").ap()
    clt_h = nc.dram_tensor("clt", [256, 256], F32, kind="ExternalInput").ap()
    alt_h = nc.dram_tensor("alt", [256, 256], F32, kind="ExternalInput").ap()
    rl_h = nc.dram_tensor("rl", [P, 1], F32, kind="ExternalInput").ap()
    flag_h = nc.dram_tensor("flag", [P, 1], F32, kind="ExternalInput").ap()
    ident_h = nc.dram_tensor("identb", [P, P], BF16, kind="ExternalInput").ap()
    zc_h = nc.dram_tensor("zc", [1, 1], BF16, kind="ExternalInput").ap()
    wg_h = nc.dram_tensor("wg", [DM, II], F32, kind="ExternalInput").ap()
    wu_h = nc.dram_tensor("wu", [DM, II], F32, kind="ExternalInput").ap()
    wd_h = nc.dram_tensor("wd", [II, DM], F32, kind="ExternalInput").ap()
    out_h = nc.dram_tensor("out", [RPC, DM], F32, kind="ExternalOutput").ap()

    xspill_h = nc.dram_tensor("xspill", [RPC, DM], F32, kind="Internal").ap()
    hscr_h = nc.dram_tensor("hscr", [NRT, P, II], BF16, kind="Internal").ap()
    x2t_h = nc.dram_tensor("x2t", [NRT, P, DM], BF16, kind="Internal").ap()
    ffnscr_h = nc.dram_tensor("ffnscr", [NRT, P, DM], F32, kind="Internal").ap()

    ctx0 = ExitStack()

    def sb(ctx, name, shape, dtype):
        return ctx.enter_context(nc.sbuf_tensor(name, shape, dtype))

    def psum(ctx, name, shape, dtype):
        return ctx.enter_context(nc.psum_tensor(name, shape, dtype))

    def mksem(name):
        return Sem(ctx0.enter_context(nc.semaphore(name)))

    sem_c = mksem("sem_c")
    sem_zc = mksem("sem_zc")
    sem_xa = [mksem(f"sem_xa{i}") for i in range(3)]
    sem_xb = [mksem(f"sem_xb{i}") for i in range(4)]
    sem_sp = [mksem(f"sem_sp{i}") for i in range(2)]
    sem_s2 = [mksem(f"sem_s2{i}") for i in range(2)]
    sem_x2r = mksem("sem_x2r")
    sem_hw = [mksem(f"sem_hw{i}") for i in range(2)]
    sem_hr = [mksem(f"sem_hr{i}") for i in range(2)]
    sem_fw = [mksem(f"sem_fw{i}") for i in range(4)]
    sem_f3 = [mksem(f"sem_f3{i}") for i in range(4)]
    sem_o = [mksem(f"sem_o{i}") for i in range(4)]
    sem_wg = [mksem(f"sem_wg{i}") for i in range(2)]
    sem_wd = [mksem(f"sem_wd{i}") for i in range(2)]
    pe = Sem(ctx0.enter_context(nc.semaphore("pe")))
    v = Sem(ctx0.enter_context(nc.semaphore("v")))
    s = Sem(ctx0.enter_context(nc.semaphore("s")))
    sems = {"pe": pe, "v": v, "s": s}

    ident_sb = sb(ctx0, "ident_sb", [P, P], BF16)
    rl_sb = sb(ctx0, "rl_sb", [P, 1], F32)
    flag_sb = sb(ctx0, "flag_sb", [P, 1], F32)
    rl1f_sb = sb(ctx0, "rl1f_sb", [P, 1], F32)
    f1m_sb = sb(ctx0, "f1m_sb", [P, 1], F32)
    eps_sb = sb(ctx0, "eps_sb", [P, 1], F32)
    st1 = sb(ctx0, "st1", [P, 2 * NRT], F32)
    strep = sb(ctx0, "strep", [P, 4], F32)
    stA = sb(ctx0, "stA", [P, 8], F32)
    lnsc = LNScratch(nc, ctx0)
    eps_ap = eps_sb[:, 0:1]

    ctx_ab = ExitStack()
    xln1t = sb(ctx_ab, "xln1t", [P, NRT * DM], BF16)
    xrept = sb(ctx_ab, "xrept", [P, 2 * DM], BF16)
    alt_b = sb(ctx_ab, "alt_b", [P, 512], BF16)
    att_sb = sb(ctx_ab, "att_sb", [P, NSUB * DM], BF16)

    ctx_uw = ExitStack()
    U_sb = sb(ctx_uw, "U_sb", [P, 2 * DM], BF16)
    W_sb = sb(ctx_uw, "W_sb", [P, 2 * DM], BF16)

    ctx_a = ExitStack()
    NXA = 3
    xa = [sb(ctx_a, f"xa{i}", [P, DM], F32) for i in range(NXA)]
    xln1s = [sb(ctx_a, f"xln1s{i}", [P, DM], BF16) for i in range(2)]
    xln1s_slot = [Slot(), Slot()]
    xrep_sb = sb(ctx_a, "xrep_sb", [P, 2 * DM], F32)
    xrep_lnb = sb(ctx_a, "xrep_lnb", [P, 2 * DM], BF16)
    blt_f = sb(ctx_a, "blt_f", [P, 512], F32)
    clt_f = sb(ctx_a, "clt_f", [P, 512], F32)
    alt_f = sb(ctx_a, "alt_f", [P, 512], F32)
    blt_b = sb(ctx_a, "blt_b", [P, 512], BF16)
    clt_b = sb(ctx_a, "clt_b", [P, 512], BF16)
    psu = [psum(ctx_a, f"psua{i}", [P, 512], F32) for i in range(2)]
    pst = [psum(ctx_a, f"psta{i}", [P, 1024], BF16) for i in range(2)]
    NB1 = 4
    pss = [psum(ctx_a, f"pss{i}", [P, 512], F32) for i in range(NB1)]

    ph = Ph(nc)
    ph.emit("sync", lambda e: e.dma_start(out=ident_sb[:, :], in_=ident_h), inc=sem_c, amt=16)
    ph.emit("sync", lambda e: e.dma_start(out=rl_sb[:, :], in_=rl_h), inc=sem_c, amt=16)
    ph.emit("sync", lambda e: e.dma_start(out=flag_sb[:, :], in_=flag_h), inc=sem_c, amt=16)
    ph.emit("sync", lambda e: e.dma_start(
        out=blt_f[:, :].rearrange("p (c i) -> p c i", i=256),
        in_=blt_h.rearrange("(c p) i -> p c i", c=2)), inc=sem_c, amt=16)
    ph.emit("sync", lambda e: e.dma_start(
        out=clt_f[:, :].rearrange("p (c i) -> p c i", i=256),
        in_=clt_h.rearrange("(c p) i -> p c i", c=2)), inc=sem_c, amt=16)
    ph.emit("sync", lambda e: e.dma_start(
        out=alt_f[:, :].rearrange("p (c i) -> p c i", i=256),
        in_=alt_h.rearrange("(c p) i -> p c i", c=2)), inc=sem_c, amt=16)
    d_consts = ph.emit("sync", lambda e: e.dma_start(
        out=xrep_sb[:, :].rearrange("p (c n) -> p c n", c=2),
        in_=xrep_h.rearrange("(c p) n -> p c n", c=2)), inc=sem_c, amt=16)

    ph.emit("vector", lambda e: e.memset(eps_sb[:, :], EPS))
    fv = ph.emit("vector", lambda e: e.tensor_scalar(
        f1m_sb[:, :], flag_sb[:, :], -1.0, 1.0, ALU.mult, ALU.add),
        waits=[(sem_c, d_consts)], inc=v)
    ph.emit("vector", lambda e: e.tensor_mul(rl1f_sb[:, :], rl_sb[:, :], f1m_sb[:, :]),
            waits=[(v, fv)])

    s_cast = None
    ph.emit("scalar", lambda e: e.copy(blt_b[:, :], blt_f[:, :]), waits=[(sem_c, d_consts)], inc=s)
    ph.emit("scalar", lambda e: e.copy(clt_b[:, :], clt_f[:, :]), inc=s)
    s_cast = ph.emit("scalar", lambda e: e.copy(alt_b[:, :], alt_f[:, :]), inc=s)

    v_replb = [None, None]
    tok_rep = [None, None]
    for jc in range(2):
        x_ap = xrep_sb[:, jc * DM : (jc + 1) * DM]
        tok_rep[jc] = ln_sums(ph, lnsc, sems, x_ap,
                              waits_v=[(sem_c, d_consts)], waits_s=[(sem_c, d_consts)])

    xa_slot = [Slot() for _ in range(NXA)]
    tok_rt = [None] * NRT
    d_xa = [None] * NRT

    def emit_load_sums(rt):
        slot = xa_slot[rt % NXA]
        d_xa[rt] = ph.emit("sync", lambda e, rt=rt: e.dma_start(
            out=xa[rt % NXA][:, :], in_=xrows_h[rt * P : (rt + 1) * P, :]),
            waits=[slot.free] if slot.free else [], inc=sem_xa[rt % NXA], amt=16)
        tok_rt[rt] = ln_sums(ph, lnsc, sems, xa[rt % NXA][:, :],
                             waits_v=[(sem_xa[rt % NXA], d_xa[rt])],
                             waits_s=[(sem_xa[rt % NXA], d_xa[rt])])

    def emit_repl_fin(jc):
        mean_ap = strep[:, 2 * jc : 2 * jc + 1]
        inv_ap = strep[:, 2 * jc + 1 : 2 * jc + 2]
        x_ap = xrep_sb[:, jc * DM : (jc + 1) * DM]
        ln_fin(ph, lnsc, sems, eps_ap, tok_rep[jc], mean_ap, inv_ap)
        v_replb[jc] = ph.emit("vector", lambda e, x_ap=x_ap, m=mean_ap, iv=inv_ap, jc=jc: e.tensor_scalar(
            xrep_lnb[:, jc * DM : (jc + 1) * DM], x_ap, m, iv, ALU.subtract, ALU.mult),
            inc=v)

    emit_load_sums(0)
    emit_repl_fin(0)
    emit_load_sums(1)
    emit_repl_fin(1)

    pst_slot = [Slot(), Slot()]
    for jc in range(2):
        for half in range(2):
            bank = pst[half % 2]
            slot = pst_slot[half % 2]

            def tgrp(e, jc=jc, half=half, bank=bank):
                last = None
                for t in range(8):
                    ns = half * 8 + t
                    last = e.transpose(
                        bank[:, t * P : (t + 1) * P],
                        xrep_lnb[:, jc * DM + ns * P : jc * DM + (ns + 1) * P],
                        ident_sb[:, :],
                    )
                return last

            g = ph.emit("tensor", tgrp,
                        waits=[(v, v_replb[jc]), (sem_c, d_consts)] + ([slot.free] if slot.free else []),
                        inc=pe)
            sc = ph.emit("scalar", lambda e, jc=jc, half=half, bank=bank: e.copy(
                xrept[:, jc * DM + half * 1024 : jc * DM + (half + 1) * 1024], bank[:, :]),
                waits=[(pe, g)], inc=s)
            slot.free = (s, sc)

    psu_slot = [Slot(), Slot()]
    s_uw = {}
    for mi, (mat_b, dst) in enumerate(((blt_b, U_sb), (clt_b, W_sb))):
        for ic in range(2):
            for nb in range(4):
                k = mi * 8 + ic * 4 + nb
                bank = psu[k % 2]
                slot = psu_slot[k % 2]

                def mmg(e, mat_b=mat_b, ic=ic, nb=nb, bank=bank):
                    last = None
                    for jc in range(2):
                        last = e.matmul(
                            bank[:, :],
                            mat_b[:, jc * 256 + ic * P : jc * 256 + (ic + 1) * P],
                            xrep_lnb[:, jc * DM + nb * 512 : jc * DM + (nb + 1) * 512],
                            start=(jc == 0), stop=(jc == 1),
                        )
                    return last

                g = ph.emit("tensor", mmg,
                            waits=[(s, s_cast), (v, v_replb[1])] + ([slot.free] if slot.free else []),
                            inc=pe)
                sc = ph.emit("scalar", lambda e, dst=dst, ic=ic, nb=nb, bank=bank: e.copy(
                    dst[:, ic * DM + nb * 512 : ic * DM + (nb + 1) * 512], bank[:, :]),
                    waits=[(pe, g)], inc=s)
                s_uw[(mi, ic, nb)] = sc
                slot.free = (s, sc)

    def emit_rt_post(rt):
        mean_ap = st1[:, 2 * rt : 2 * rt + 1]
        inv_ap = st1[:, 2 * rt + 1 : 2 * rt + 2]
        ln_fin(ph, lnsc, sems, eps_ap, tok_rt[rt], mean_ap, inv_ap)
        x1sl = xln1s_slot[rt % 2]
        vm = ph.emit("vector", lambda e, rt=rt, m=mean_ap, iv=inv_ap: e.tensor_scalar(
            xln1s[rt % 2][:, :], xa[rt % NXA][:, :], m, iv, ALU.subtract, ALU.mult),
            waits=[x1sl.free] if x1sl.free else [], inc=v)
        xa_slot[rt % NXA].free = (v, vm)
        for half in range(2):
            bank = pst[half % 2]
            tslot = pst_slot[half % 2]

            def tgrp(e, rt=rt, half=half, bank=bank):
                last = None
                for t in range(8):
                    ns = half * 8 + t
                    last = e.transpose(
                        bank[:, t * P : (t + 1) * P],
                        xln1s[rt % 2][:, ns * P : (ns + 1) * P],
                        ident_sb[:, :],
                    )
                return last

            g = ph.emit("tensor", tgrp,
                        waits=[(v, vm)] + ([tslot.free] if tslot.free else []), inc=pe)
            x1sl.free = (pe, g)
            sc = ph.emit("scalar", lambda e, rt=rt, half=half, bank=bank: e.copy(
                xln1t[:, rt * DM + half * 1024 : rt * DM + (half + 1) * 1024], bank[:, :]),
                waits=[(pe, g)], inc=s)
            tslot.free = (s, sc)

    emit_rt_post(0)
    for rt in range(2, NRT):
        emit_load_sums(rt)
        emit_rt_post(rt - 1)
    emit_rt_post(NRT - 1)

    pss_slot = [Slot() for _ in range(NB1)]
    s_relu_last = None
    v_relu_last = None
    for mb in range(NMB):
        for nt in range(NSUB):
            k = mb * NSUB + nt
            bank = pss[k % NB1]
            slot = pss_slot[k % NB1]
            uw_wait = max(s_uw[(0, 0, nt // 4)], s_uw[(0, 1, nt // 4)],
                          s_uw[(1, 0, mb)], s_uw[(1, 1, mb)])

            def mmg(e, nt=nt, mb=mb, bank=bank):
                last = None
                for dc in range(2):
                    last = e.matmul(
                        bank[:, :],
                        U_sb[:, dc * DM + nt * P : dc * DM + (nt + 1) * P],
                        W_sb[:, dc * DM + mb * MBS : dc * DM + (mb + 1) * MBS],
                        start=(dc == 0), stop=(dc == 1),
                    )
                return last

            g = ph.emit("tensor", mmg,
                        waits=[(s, uw_wait)] + ([slot.free] if slot.free else []),
                        inc=pe)
            dst = att_sb[:, nt * DM + mb * MBS : nt * DM + (mb + 1) * MBS]
            if k % 2 == 0:
                sc = ph.emit("scalar", lambda e, dst=dst, bank=bank: e.activation(
                    dst, bank[:, :], ACT.Relu), waits=[(pe, g)], inc=s)
                slot.free = (s, sc)
                s_relu_last = sc
            else:
                vc = ph.emit("vector", lambda e, dst=dst, bank=bank: e.tensor_relu(
                    dst, bank[:, :]), waits=[(pe, g)], inc=v)
                slot.free = (v, vc)
                v_relu_last = vc
    dz = ph.emit("sync", lambda e: e.dma_start(
        out=att_sb[127:128, 15 * DM + 2047 : 15 * DM + 2048], in_=zc_h[0:1, 0:1]),
        waits=[(s, s_relu_last), (v, v_relu_last)], inc=sem_zc, amt=16)
    ph.emit("sync", lambda e: None, waits=[(sem_zc, dz)])
    ph.run()
    ctx_a.close()
    ctx_uw.close()

    ctx_b2 = ExitStack()
    xb = [sb(ctx_b2, f"xb{i}", [P, DM], F32) for i in range(4)]
    amid = [sb(ctx_b2, f"amid{i}", [P, DM], F32) for i in range(2)]
    xatt_b = sb(ctx_b2, "xatt_b", [P, 2 * DM], BF16)
    yln = sb(ctx_b2, "yln", [P, 2 * MBS], F32)
    xln2s = [sb(ctx_b2, f"xln2s{i}", [P, DM], BF16) for i in range(2)]
    xln2s_slot = [Slot(), Slot()]
    stage2 = [sb(ctx_b2, f"stage2{i}", [P, 1024], BF16) for i in range(2)]
    st2 = sb(ctx_b2, "st2", [P, 8], F32)
    psy = [psum(ctx_b2, f"psy{i}", [P, 512], F32) for i in range(3)]
    pst2 = [psum(ctx_b2, f"pst2{i}", [P, 1024], BF16) for i in range(2)]

    ph = Ph(nc)
    psy_slot = [Slot(), Slot(), Slot()]
    pst2_slot = [Slot(), Slot()]
    xb_slot = [Slot() for _ in range(4)]
    psy_i = 0
    d_xb = [None] * (NRT + 4)

    def load_xb(rt):
        slot = xb_slot[rt % 4]
        d_xb[rt] = ph.emit("sync", lambda e, rt=rt: e.dma_start(
            out=xb[rt % 4][:, :], in_=xrows_h[rt * P : (rt + 1) * P, :]),
            waits=[slot.free] if slot.free else [], inc=sem_xb[rt % 4], amt=16)

    for _rt in range(4):
        load_xb(_rt)

    v_amid = [[None] * NMB for _ in range(NRT)]
    v_amid_fin = [None] * NRT
    s_xatt = [[None] * NMB for _ in range(2)]
    stage2_slots = [Slot(), Slot()]

    def emit_y(rt):
        nonlocal psy_i
        rl_scal = rl1f_sb if rt < 2 else rl_sb
        for mb in range(NMB):
            bank = psy[psy_i % 3]
            slot = psy_slot[psy_i % 3]
            psy_i += 1

            def mmg(e, rt=rt, mb=mb, bank=bank):
                last = None
                for ns in range(NSUB):
                    last = e.matmul(
                        bank[:, :],
                        xln1t[:, rt * DM + ns * P : rt * DM + (ns + 1) * P],
                        att_sb[:, ns * DM + mb * MBS : ns * DM + (mb + 1) * MBS],
                        start=(ns == 0), stop=(ns == NSUB - 1),
                    )
                return last

            g = ph.emit("tensor", mmg, waits=[slot.free] if slot.free else [], inc=pe)
            yl = yln[:, (mb % 2) * MBS : (mb % 2 + 1) * MBS]
            vy = ph.emit("vector", lambda e, rt=rt, mb=mb, yl=yl: e.tensor_scalar(
                yl, xb[rt % 4][:, mb * MBS : (mb + 1) * MBS],
                st1[:, 2 * rt : 2 * rt + 1], st1[:, 2 * rt + 1 : 2 * rt + 2],
                ALU.subtract, ALU.mult),
                waits=[(sem_xb[rt % 4], d_xb[rt])], inc=v)
            aw = [(pe, g), (v, vy)]
            if rt >= 2 and tokA[rt - 2] is not None:
                aw.append((s, tokA[rt - 2][1]))
            vm = ph.emit("vector", lambda e, rt=rt, mb=mb, bank=bank, rl_scal=rl_scal, yl=yl: e.scalar_tensor_tensor(
                amid[rt % 2][:, mb * MBS : (mb + 1) * MBS], bank[:, :], rl_scal[:, 0:1],
                yl, ALU.mult, ALU.add),
                waits=aw, inc=v)
            v_amid[rt][mb] = vm
            v_amid_fin[rt] = vm
            slot.free = (v, vm)

    def emit_xatt(jc):
        nonlocal psy_i
        for mb in range(NMB):
            bank = psy[psy_i % 3]
            slot = psy_slot[psy_i % 3]
            psy_i += 1

            def mmg(e, jc=jc, mb=mb, bank=bank):
                last = None
                for ns in range(NSUB):
                    last = e.matmul(
                        bank[:, :],
                        xrept[:, jc * DM + ns * P : jc * DM + (ns + 1) * P],
                        att_sb[:, ns * DM + mb * MBS : ns * DM + (mb + 1) * MBS],
                        start=(ns == 0), stop=(ns == NSUB - 1),
                    )
                return last

            g = ph.emit("tensor", mmg, waits=[slot.free] if slot.free else [], inc=pe)
            sc = ph.emit("scalar", lambda e, jc=jc, mb=mb, bank=bank: e.copy(
                xatt_b[:, jc * DM + mb * MBS : jc * DM + (mb + 1) * MBS], bank[:, :]),
                waits=[(pe, g)], inc=s)
            s_xatt[jc][mb] = sc
            slot.free = (s, sc)

    def emit_al():
        nonlocal psy_i
        for ic in range(2):
            for mb in range(NMB):
                bank = psy[psy_i % 3]
                slot = psy_slot[psy_i % 3]
                psy_i += 1

                def mmg(e, ic=ic, mb=mb, bank=bank):
                    last = None
                    for jc in range(2):
                        last = e.matmul(
                            bank[:, :],
                            alt_b[:, jc * 256 + ic * P : jc * 256 + (ic + 1) * P],
                            xatt_b[:, jc * DM + mb * MBS : jc * DM + (mb + 1) * MBS],
                            start=(jc == 0), stop=(jc == 1),
                        )
                    return last

                g = ph.emit("tensor", mmg,
                            waits=[(s, s_xatt[1][mb])] + ([slot.free] if slot.free else []),
                            inc=pe)
                vm = ph.emit("vector", lambda e, ic=ic, mb=mb, bank=bank: e.scalar_tensor_tensor(
                    amid[ic][:, mb * MBS : (mb + 1) * MBS], bank[:, :], flag_sb[:, 0:1],
                    amid[ic][:, mb * MBS : (mb + 1) * MBS], ALU.mult, ALU.add),
                    waits=[(pe, g)], inc=v)
                v_amid_fin[ic] = vm
                slot.free = (v, vm)

    tokA = [None] * NRT
    tok2 = [None] * NRT

    def emit_p1(rt):
        tokA[rt] = ln_sums(ph, lnsc, sems, amid[rt % 2][:, :],
                           waits_s=[(v, v_amid_fin[rt])])

    def emit_p2(rt):
        a_ap = amid[rt % 2][:, :]
        sA = stA[:, 2 * (rt % 4) : 2 * (rt % 4) + 1]
        sAi = stA[:, 2 * (rt % 4) + 1 : 2 * (rt % 4) + 2]
        ln_fin(ph, lnsc, sems, eps_ap, tokA[rt], sA, sAi)
        vn = ph.emit("vector", lambda e, a_ap=a_ap, sA=sA, sAi=sAi: e.tensor_scalar(
            a_ap, a_ap, sA, sAi, ALU.subtract, ALU.mult), inc=v)
        vx = ph.emit("vector", lambda e, rt=rt, a_ap=a_ap: e.tensor_add(
            xb[rt % 4][:, :], xb[rt % 4][:, :], a_ap), waits=[(v, vn)], inc=v)
        tok2[rt] = ln_sums(ph, lnsc, sems, xb[rt % 4][:, :], waits_s=[(v, vx)])

    def emit_p3(rt):
        s2m = st2[:, 2 * (rt % 4) : 2 * (rt % 4) + 1]
        s2i = st2[:, 2 * (rt % 4) + 1 : 2 * (rt % 4) + 2]
        _, sq2 = ln_fin(ph, lnsc, sems, eps_ap, tok2[rt], s2m, s2i)
        sq2 = tok2[rt][1]
        xsl = xln2s_slot[rt % 2]
        vm = ph.emit("vector", lambda e, rt=rt, s2m=s2m, s2i=s2i: e.tensor_scalar(
            xln2s[rt % 2][:, :], xb[rt % 4][:, :], s2m, s2i,
            ALU.subtract, ALU.mult),
            waits=[xsl.free] if xsl.free else [], inc=v)
        dsp = ph.emit("gpsimd", lambda e, rt=rt: e.dma_start(
            out=xspill_h[rt * P : (rt + 1) * P, :], in_=xb[rt % 4][:, :]),
            waits=[(v, vm), (s, sq2)], inc=sem_sp[rt % 2], amt=16)
        xb_slot[rt % 4].free = (sem_sp[rt % 2], dsp)
        if rt + 4 < NRT:
            load_xb(rt + 4)
        for half in range(2):
            bank = pst2[half % 2]
            tslot = pst2_slot[half % 2]

            def tgrp(e, rt=rt, half=half, bank=bank):
                last = None
                for t in range(8):
                    ns = half * 8 + t
                    last = e.transpose(
                        bank[:, t * P : (t + 1) * P],
                        xln2s[rt % 2][:, ns * P : (ns + 1) * P],
                        ident_sb[:, :],
                    )
                return last

            g = ph.emit("tensor", tgrp,
                        waits=[(v, vm)] + ([tslot.free] if tslot.free else []), inc=pe)
            xsl.free = (pe, g)
            stg_slot = stage2[half % 2]
            sslot = stage2_slots[half % 2]
            sc = ph.emit("scalar", lambda e, half=half, bank=bank, stg=stg_slot: e.copy(
                stg[:, :], bank[:, :]),
                waits=[(pe, g)] + ([sslot.free] if sslot.free else []), inc=s)
            tslot.free = (s, sc)
            dd = ph.emit("gpsimd", lambda e, rt=rt, half=half, stg=stg_slot: e.dma_start(
                out=x2t_h[rt][:, half * 1024 : (half + 1) * 1024], in_=stg[:, :]),
                waits=[(s, sc)], inc=sem_s2[half % 2], amt=16)
            sslot.free = (sem_s2[half % 2], dd)

    emit_y(0)
    emit_xatt(0)
    emit_y(1)
    emit_xatt(1)
    emit_al()
    emit_p1(0)
    emit_p1(1)
    emit_p2(0)
    emit_y(2)
    emit_p1(2)
    for k in range(3, NRT):
        emit_p3(k - 3)
        emit_p2(k - 2)
        emit_y(k)
        emit_p1(k)
    emit_p2(NRT - 2)
    emit_p3(NRT - 3)
    emit_p2(NRT - 1)
    emit_p3(NRT - 2)
    emit_p3(NRT - 1)
    ph.emit("sync", lambda e: None,
            waits=[(sem_sp[0], sem_sp[0].n), (sem_sp[1], sem_sp[1].n),
                   (sem_s2[0], sem_s2[0].n), (sem_s2[1], sem_s2[1].n)])
    ph.run()
    ctx_b2.close()
    ctx_ab.close()

    ctx_f12 = ExitStack()
    wdbuf0 = sb(ctx_f12, "wdbuf0", [P, ISUBS * 512], BF16)
    ctx_f1 = ExitStack()
    xln2t = sb(ctx_f1, "xln2t", [P, NRT * DM], BF16)
    wgbuf = [sb(ctx_f1, f"wgbuf{i}", [P, NSUB * 512], BF16) for i in range(2)]
    wubuf = [sb(ctx_f1, f"wubuf{i}", [P, NSUB * 512], BF16) for i in range(2)]
    sbg = [sb(ctx_f1, f"sbg{i}", [P, ICS], F32) for i in range(2)]
    sbh = [sb(ctx_f1, f"sbh{i}", [P, ICS], F32) for i in range(2)]
    sba = [sb(ctx_f1, f"sba{i}", [P, ICS], F32) for i in range(2)]
    maskb = sb(ctx_f1, "maskb", [P, ICS], F32)
    Mb = sb(ctx_f1, "Mb", [P, ICS // 2], F32)
    mb_ = sb(ctx_f1, "mb_", [P, ICS // 2], F32)
    mmb = sb(ctx_f1, "mmb", [P, ICS // 4], F32)
    mxb = sb(ctx_f1, "mxb", [P, ICS // 4], F32)
    t2b = sb(ctx_f1, "t2b", [P, ICS // 4], F32)
    hsp = [sb(ctx_f1, f"hsp{i}", [P, ICS], BF16) for i in range(2)]
    hst = [sb(ctx_f1, f"hst{i}", [P, ICS], BF16) for i in range(2)]
    psg = [psum(ctx_f1, f"psg{i}", [P, 512], F32) for i in range(2)]
    psup = [psum(ctx_f1, f"psup{i}", [P, 512], F32) for i in range(2)]
    pstt = [psum(ctx_f1, f"pstt{i}", [P, 512], BF16) for i in range(2)]

    ph = Ph(nc)
    d_x2t_all = ph.emit("sync", lambda e: e.dma_start(
        out=xln2t[:, :].rearrange("p (r f) -> p r f", f=DM),
        in_=x2t_h.rearrange("r p f -> p r f")),
        waits=[(sem_s2[0], sem_s2[0].n), (sem_s2[1], sem_s2[1].n)],
        inc=sem_x2r, amt=16)

    NK = NIC * NRT
    snap_w = [None] * NIC
    peG = [None] * NK
    peU = [None] * NK
    peT = [None] * NK
    s_silu = [None] * NK
    s_abs = [None] * NK
    s_hst = [None] * NK
    v_h = [None] * NK
    v_hsp = [None] * NK
    d_hst = [None] * NK

    def emit_transposes(k):
        bank = pstt[k % 2]
        w = [(v, v_hsp[k])]
        if k - 2 >= 0:
            w.append((s, s_hst[k - 2]))

        def tgrp(e, k=k, bank=bank):
            last = None
            for q in range(NQ):
                last = e.transpose(
                    bank[:, q * P : (q + 1) * P],
                    hsp[k % 2][:, q * P : (q + 1) * P],
                    ident_sb[:, :],
                )
            return last

        peT[k] = ph.emit("tensor", tgrp, waits=w, inc=pe)

    def emit_hstage(k):
        wk = [(pe, peT[k])]
        if k - 2 >= 0:
            wk.append((sem_hw[k % 2], d_hst[k - 2]))
        s_hst[k] = ph.emit("scalar", lambda e, k=k: e.copy(hst[k % 2][:, :], pstt[k % 2][:, :]),
                           waits=wk, inc=s)

    def emit_hdma(k):
        ic, rt = divmod(k, NRT)
        d_hst[k] = ph.emit("sync", lambda e, ic=ic, rt=rt, k=k: e.dma_start(
            out=hscr_h[rt][:, ic * ICS : (ic + 1) * ICS], in_=hst[k % 2][:, :]),
            waits=[(s, s_hst[k])], inc=sem_hw[k % 2], amt=16)

    for ic in range(NIC):
        wslot_wait = []
        if ic - 2 >= 0:
            wslot_wait = [(pe, peU[(ic - 2) * NRT + NRT - 1])]
        ph.emit("gpsimd", lambda e, ic=ic: e.dma_start(
            out=wgbuf[ic % 2][:, :].rearrange("p (c f) -> p c f", f=512),
            in_=wg_h[:, ic * ICS : (ic + 1) * ICS].rearrange("(c p) f -> p c f", p=P)),
            waits=wslot_wait, inc=sem_wg[ic % 2], amt=16)
        snap_w[ic] = ph.emit("gpsimd", lambda e, ic=ic: e.dma_start(
            out=wubuf[ic % 2][:, :].rearrange("p (c f) -> p c f", f=512),
            in_=wu_h[:, ic * ICS : (ic + 1) * ICS].rearrange("(c p) f -> p c f", p=P)),
            inc=sem_wg[ic % 2], amt=16)

        for rt in range(NRT):
            k = ic * NRT + rt
            wg_waits = [(sem_wg[ic % 2], snap_w[ic])]
            if k == 0:
                wg_waits.append((sem_x2r, d_x2t_all))
            if k - 2 >= 0:
                wg_waits.append((s, s_silu[k - 2]))

            def gate(e, ic=ic, rt=rt, k=k):
                last = None
                for ns in range(NSUB):
                    last = e.matmul(
                        psg[k % 2][:, :],
                        xln2t[:, rt * DM + ns * P : rt * DM + (ns + 1) * P],
                        wgbuf[ic % 2][:, ns * 512 : (ns + 1) * 512],
                        start=(ns == 0), stop=(ns == NSUB - 1),
                    )
                return last

            peG[k] = ph.emit("tensor", gate, waits=wg_waits, inc=pe)

            wu_waits = []
            if k - 2 >= 0:
                wu_waits.append((v, v_h[k - 2]))

            def up(e, ic=ic, rt=rt, k=k):
                last = None
                for ns in range(NSUB):
                    last = e.matmul(
                        psup[k % 2][:, :],
                        xln2t[:, rt * DM + ns * P : rt * DM + (ns + 1) * P],
                        wubuf[ic % 2][:, ns * 512 : (ns + 1) * 512],
                        start=(ns == 0), stop=(ns == NSUB - 1),
                    )
                return last

            peU[k] = ph.emit("tensor", up, waits=wu_waits, inc=pe)
            if k - 1 >= 0:
                emit_transposes(k - 1)

            sw = [(pe, peG[k])]
            if k - 2 >= 0:
                sw.append((v, v_h[k - 2]))
            s_silu[k] = ph.emit("scalar", lambda e, k=k: e.activation(
                sbg[k % 2][:, :], psg[k % 2][:, :], ACT.Silu), waits=sw, inc=s)
            v_h[k] = ph.emit("vector", lambda e, k=k: e.tensor_mul(
                sbh[k % 2][:, :], sbg[k % 2][:, :], psup[k % 2][:, :]),
                waits=[(s, s_silu[k]), (pe, peU[k])], inc=v)
            s_abs[k] = ph.emit("scalar", lambda e, k=k: e.activation(
                sba[k % 2][:, :], sbh[k % 2][:, :], ACT.Abs), waits=[(v, v_h[k])], inc=s)
            if k - 1 >= 0:
                emit_hstage(k - 1)

            a = sba[k % 2]
            spw = [(s, s_abs[k])]
            if k - 2 >= 0:
                spw.append((pe, peT[k - 2]))
            vM = ph.emit("vector", lambda e, a=a: e.tensor_tensor(
                Mb[:, :], a[:, 0:ICS:2], a[:, 1:ICS:2], ALU.max), waits=spw, inc=v)
            vm_ = ph.emit("vector", lambda e, a=a: e.tensor_tensor(
                mb_[:, :], a[:, 0:ICS:2], a[:, 1:ICS:2], ALU.min), inc=v)
            ph.emit("vector", lambda e: e.tensor_tensor(
                mmb[:, :], Mb[:, 0 : ICS // 2 : 2], Mb[:, 1 : ICS // 2 : 2], ALU.min),
                waits=[(v, vM)])
            vmx = ph.emit("vector", lambda e: e.tensor_tensor(
                mxb[:, :], mb_[:, 0 : ICS // 2 : 2], mb_[:, 1 : ICS // 2 : 2], ALU.max),
                waits=[(v, vm_)], inc=v)
            vt2 = ph.emit("vector", lambda e: e.tensor_tensor(
                t2b[:, :], mmb[:, :], mxb[:, :], ALU.max), waits=[(v, vmx)], inc=v)
            vge = ph.emit("vector", lambda e, a=a: e.tensor_tensor(
                maskb[:, :].rearrange("p (g q) -> p g q", q=4),
                a[:, :].rearrange("p (g q) -> p g q", q=4),
                t2b[:, :].unsqueeze(2).broadcast_to([P, ICS // 4, 4]),
                ALU.is_ge), waits=[(v, vt2)], inc=v)
            v_hsp[k] = ph.emit("vector", lambda e, k=k: e.tensor_mul(
                hsp[k % 2][:, :], sbh[k % 2][:, :], maskb[:, :]),
                waits=[(v, vge)], inc=v)
            if k - 1 >= 0:
                emit_hdma(k - 1)

    k = NK - 1
    emit_transposes(k)
    emit_hstage(k)
    emit_hdma(k)
    snap_wd_pre = None
    for t in range(4):
        snap_wd_pre = ph.emit("gpsimd", lambda e, t=t: e.dma_start(
            out=wdbuf0[:, t * 16 * 512 : (t + 1) * 16 * 512].rearrange(
                "p (c f) -> p c f", f=512),
            in_=wd_h[t * 2048 : (t + 1) * 2048, 0:OBS].rearrange(
                "(c p) f -> p c f", p=P)),
            inc=sem_wd[0], amt=16)
    ph.emit("sync", lambda e: None,
            waits=[(sem_hw[0], sem_hw[0].n), (sem_hw[1], sem_hw[1].n)])
    ph.run()
    ctx_f1.close()

    ctx_f2 = ExitStack()
    wdbuf = [wdbuf0, sb(ctx_f2, "wdbuf1", [P, ISUBS * 512], BF16)]
    hrt = [sb(ctx_f2, f"hrt{i}", [P, II], BF16) for i in range(2)]
    ostage = [sb(ctx_f2, f"ostage{i}", [P, OBS], F32) for i in range(4)]
    psd = [psum(ctx_f2, f"psd{i}", [P, 512], F32) for i in range(4)]
    xt = [sb(ctx_f2, f"xt{i}", [P, DM], F32) for i in range(2)]
    ft = [sb(ctx_f2, f"ft{i}", [P, DM], F32) for i in range(2)]
    stC = sb(ctx_f2, "stC", [P, 8], F32)

    ph = Ph(nc)
    v_out = [None] * NRT
    d_out = [None] * NRT
    sq_f3 = [None] * NRT
    tokC = [None] * NRT
    d_ft = [None] * NRT

    def f3_load(rt, fw_wait):
        w = [fw_wait]
        if rt == 0:
            w += [(sem_sp[0], sem_sp[0].n), (sem_sp[1], sem_sp[1].n)]
        if rt - 2 >= 0:
            w += [(v, v_out[rt - 2]), (s, sq_f3[rt - 2]),
                  (sem_o[(rt - 2) % 4], d_out[rt - 2])]
        ph.emit("gpsimd", lambda e, rt=rt: e.dma_start(
            out=xt[rt % 2][:, :], in_=xspill_h[rt * P : (rt + 1) * P, :]),
            waits=w, inc=sem_f3[rt % 2], amt=16)
        d_ft[rt] = ph.emit("gpsimd", lambda e, rt=rt: e.dma_start(
            out=ft[rt % 2][:, :], in_=ffnscr_h[rt]), inc=sem_f3[rt % 2], amt=16)

    def f3_sums(rt):
        tokC[rt] = ln_sums(ph, lnsc, sems, ft[rt % 2][:, :],
                           waits_v=[(sem_f3[rt % 2], d_ft[rt])],
                           waits_s=[(sem_f3[rt % 2], d_ft[rt])])
        sq_f3[rt] = tokC[rt][1]

    def f3_fin(rt):
        sCm = stC[:, 2 * (rt % 4) : 2 * (rt % 4) + 1]
        sCi = stC[:, 2 * (rt % 4) + 1 : 2 * (rt % 4) + 2]
        ln_fin(ph, lnsc, sems, eps_ap, tokC[rt], sCm, sCi)
        vn3 = ph.emit("vector", lambda e, rt=rt, sCm=sCm, sCi=sCi: e.tensor_scalar(
            ft[rt % 2][:, :], ft[rt % 2][:, :], sCm, sCi,
            ALU.subtract, ALU.mult), inc=v)
        v_out[rt] = ph.emit("vector", lambda e, rt=rt: e.tensor_add(
            ft[rt % 2][:, :], ft[rt % 2][:, :], xt[rt % 2][:, :]),
            waits=[(v, vn3)], inc=v)
        d_out[rt] = ph.emit("scalar", lambda e, rt=rt: e.dma_start(
            out=out_h[rt * P : (rt + 1) * P, :], in_=ft[rt % 2][:, :]),
            waits=[(v, v_out[rt])], inc=sem_o[rt % 4], amt=16)
    NJ = NOB * NRT
    peD = [None] * NJ
    v_oev = [None] * NJ
    d_hrt = [None] * NJ
    d_ost = [None] * NJ
    snap_wd = [None] * NOB

    snap_wd[0] = snap_wd_pre
    for ob in range(NOB):
        if ob > 0:
            wsl = []
            if ob - 2 >= 0:
                wsl = [(pe, peD[(ob - 2) * NRT + NRT - 1])]
            for t in range(4):
                snap_wd[ob] = ph.emit("gpsimd", lambda e, ob=ob, t=t: e.dma_start(
                    out=wdbuf[ob % 2][:, t * 16 * 512 : (t + 1) * 16 * 512].rearrange(
                        "p (c f) -> p c f", f=512),
                    in_=wd_h[t * 2048 : (t + 1) * 2048, ob * OBS : (ob + 1) * OBS].rearrange(
                        "(c p) f -> p c f", p=P)),
                    waits=wsl if t == 0 else [], inc=sem_wd[ob % 2], amt=16)

        for rt in range(NRT):
            j = ob * NRT + rt
            hw = []
            if j - 2 >= 0:
                hw = [(pe, peD[j - 2])]
            hw0 = [(sem_hw[0], sem_hw[0].n), (sem_hw[1], sem_hw[1].n)] if j == 0 else []
            d_hrt[j] = ph.emit("sync", lambda e, rt=rt, j=j: e.dma_start(
                out=hrt[j % 2][:, :], in_=hscr_h[rt]), waits=hw + hw0,
                inc=sem_hr[j % 2], amt=16)

            mw = [(sem_wd[ob % 2], snap_wd[ob]), (sem_hr[j % 2], d_hrt[j])]
            if j - 4 >= 0:
                mw.append((v, v_oev[j - 4]))

            def down(e, ob=ob, j=j):
                last = None
                for isub in range(ISUBS):
                    last = e.matmul(
                        psd[j % 4][:, :],
                        hrt[j % 2][:, isub * P : (isub + 1) * P],
                        wdbuf[ob % 2][:, isub * 512 : (isub + 1) * 512],
                        start=(isub == 0), stop=(isub == ISUBS - 1),
                    )
                return last

            peD[j] = ph.emit("tensor", down, waits=mw, inc=pe)
            ow = [(pe, peD[j])]
            if j - 4 >= 0:
                ow.append((sem_fw[j % 4], d_ost[j - 4]))
            v_oev[j] = ph.emit("vector", lambda e, j=j: e.tensor_copy(
                ostage[j % 4][:, :], psd[j % 4][:, :]), waits=ow, inc=v)
            d_ost[j] = ph.emit("scalar", lambda e, ob=ob, rt=rt, j=j: e.dma_start(
                out=ffnscr_h[rt][:, ob * OBS : (ob + 1) * OBS], in_=ostage[j % 4][:, :]),
                waits=[(v, v_oev[j])], inc=sem_fw[j % 4], amt=16)
            if ob == NOB - 1:
                if rt >= 1:
                    f3_sums(rt - 1)
                if rt >= 2:
                    f3_fin(rt - 2)
                f3_load(rt, (sem_fw[j % 4], d_ost[j]))

    f3_sums(NRT - 1)
    f3_fin(NRT - 2)
    f3_fin(NRT - 1)
    ph.emit("sync", lambda e: None,
            waits=[(so, so.n) for so in sem_o])
    ph.run()
    ctx_f2.close()
    ctx_f12.close()

    ctx0.close()
    return nc


def kernel(L0_attn_Z_in, L0_attn_A_l, L0_attn_r_l, L0_attn_B_l, L0_attn_C_l,
           L0_ffn_w_gate, L0_ffn_w_up, L0_ffn_w_down):
    Z = np.ascontiguousarray(np.asarray(L0_attn_Z_in, dtype=np.float32).reshape(2 * 4096, DM))
    A_l = np.asarray(L0_attn_A_l, dtype=np.float32)
    B_l = np.asarray(L0_attn_B_l, dtype=np.float32)
    C_l = np.asarray(L0_attn_C_l, dtype=np.float32)
    r_l = np.asarray(L0_attn_r_l, dtype=np.float32)
    wg = np.ascontiguousarray(np.asarray(L0_ffn_w_gate, dtype=np.float32))
    wu = np.ascontiguousarray(np.asarray(L0_ffn_w_up, dtype=np.float32))
    wd = np.ascontiguousarray(np.asarray(L0_ffn_w_down, dtype=np.float32))

    blt = np.ascontiguousarray(B_l.T)
    clt = np.ascontiguousarray(C_l.T)
    alt = np.ascontiguousarray(A_l.T)
    rl_rep = np.ascontiguousarray(np.broadcast_to(r_l.reshape(1, 1), (P, 1)).astype(np.float32))
    ident = np.eye(P, dtype=ml_dtypes.bfloat16)
    zc = np.zeros((1, 1), dtype=ml_dtypes.bfloat16)

    nc = build_nc()
    in_maps = []
    for c in range(NCORES):
        b = (c * RPC) // 4096
        in_maps.append({
            "xrows": np.ascontiguousarray(Z[c * RPC : (c + 1) * RPC]),
            "xrep": np.ascontiguousarray(Z[b * 4096 : b * 4096 + 256]),
            "blt": blt, "clt": clt, "alt": alt,
            "rl": rl_rep,
            "flag": np.full((P, 1), 1.0 if c % 4 == 0 else 0.0, dtype=np.float32),
            "identb": ident, "zc": zc,
            "wg": wg, "wu": wu, "wd": wd,
        })
    res = run_bass_kernel_spmd(nc, in_maps, core_ids=list(range(NCORES)))
    outs = [np.asarray(r["out"]) for r in res.results]
    return np.concatenate(outs, axis=0).reshape(2, 4096, DM).astype(np.float32)


# version 16
# speedup vs baseline: 1.0012x; 1.0012x over previous
import numpy as np
import ml_dtypes
from contextlib import ExitStack

import concourse.bass as bass
import concourse.mybir as mybir
from concourse.bass_utils import run_bass_kernel_spmd

F32 = mybir.dt.float32
BF16 = mybir.dt.bfloat16
ALU = mybir.AluOpType
ACT = mybir.ActivationFunctionType
AX = mybir.AxisListType

P = 128
DM = 2048
NSUB = DM // P
RPC = 1024
NRT = RPC // P
MBS = 512
NMB = DM // MBS
II = 8192
ICS = 512
NIC = II // ICS
NQ = ICS // P
OBS = 512
NOB = DM // OBS
ISUBS = II // P
NCORES = 8
EPS = 1e-5

ENGS = ("sync", "gpsimd", "scalar", "vector", "tensor")


class Sem:
    def __init__(self, h):
        self.h = h
        self.n = 0


class Slot:
    def __init__(self):
        self.free = None


class Ph:
    def __init__(self, nc):
        self.nc = nc
        self.q = {e: [] for e in ENGS}

    def emit(self, eng, fn, waits=(), inc=None, amt=1):
        ws = [(sm, val) for (sm, val) in waits if sm is not None and val > 0]
        self.q[eng].append((ws, fn, inc, amt))
        if inc is not None:
            inc.n += amt
            return inc.n
        return None

    def run(self):
        with self.nc.Block() as block:
            for e in ENGS:
                items = self.q[e]
                if not items:
                    continue

                def body(eng, items=items):
                    for ws, fn, inc, amt in items:
                        for sm, val in ws:
                            eng.wait_ge(sm.h, val)
                        inst = fn(eng)
                        if inc is not None:
                            inst.then_inc(inc.h, amt)

                getattr(block, e)(body)


class LNScratch:
    NSET = 4

    def __init__(self, nc, ctx):
        t = ctx.enter_context(nc.sbuf_tensor("ln_scr", [P, 8 * self.NSET], F32))
        self.ssum = [t[:, 8 * i + 0 : 8 * i + 1] for i in range(self.NSET)]
        self.ssq = [t[:, 8 * i + 1 : 8 * i + 2] for i in range(self.NSET)]
        self.ex2 = [t[:, 8 * i + 2 : 8 * i + 3] for i in range(self.NSET)]
        self.var = [t[:, 8 * i + 4 : 8 * i + 5] for i in range(self.NSET)]
        self.std = [t[:, 8 * i + 5 : 8 * i + 6] for i in range(self.NSET)]
        ds = [ctx.enter_context(nc.sbuf_tensor(f"ln_dump{i}", [P, DM], mybir.dt.float8e4))
              for i in range(2)]
        self.dumps = [x[:, :] for x in ds]
        self.i = 0
        self.last_ex2 = [0] * self.NSET
        self.last_mean = [0] * self.NSET
        self.last_sqrt = [0] * self.NSET


def ln_sums(ph, st, sems, x_ap, waits_v=(), waits_s=()):
    v, s = sems["v"], sems["s"]
    i = st.i % st.NSET
    dump = st.dumps[st.i % 2]
    dump_b = st.dumps[(st.i + 1) % 2]
    st.i += 1
    ph.emit(
        "scalar",
        lambda e: e.activation(dump, x_ap, ACT.Square, accum_out=st.ssq[i]),
        waits=list(waits_s) + [(v, st.last_ex2[i])],
    )
    ph.emit(
        "scalar",
        lambda e: e.activation(dump_b, x_ap, ACT.Copy, accum_out=st.ssum[i]),
        waits=list(waits_v) + [(v, st.last_mean[i])],
    )
    sm = ph.emit("scalar", lambda e: e.drain(), inc=s)
    return (i, sm)


def ln_fin(ph, st, sems, eps_ap, token, mean_ap, inv_ap):
    v, s = sems["v"], sems["s"]
    i, sm = token
    ssum, ssq, ex2, var, std = st.ssum[i], st.ssq[i], st.ex2[i], st.var[i], st.std[i]
    mn = ph.emit(
        "vector",
        lambda e: e.tensor_scalar(mean_ap, ssum, 1.0 / DM, None, ALU.mult, ALU.bypass),
        waits=[(s, sm)],
        inc=v,
    )
    st.last_mean[i] = mn
    ex = ph.emit(
        "vector",
        lambda e: e.tensor_scalar(ex2, ssq, 1.0 / DM, None, ALU.mult, ALU.bypass),
        inc=v,
    )
    st.last_ex2[i] = ex
    ph.emit("vector", lambda e: None, waits=[(v, ex)])
    nv = ph.emit(
        "vector",
        lambda e: e.scalar_tensor_tensor(var, mean_ap, mean_ap, ex2,
                                         ALU.mult, ALU.subtract),
        waits=[(s, st.last_sqrt[i])],
        inc=v,
    )
    sq2 = ph.emit(
        "scalar",
        lambda e: e.activation(std, var, ACT.Sqrt, scale=-1.0, bias=eps_ap),
        waits=[(v, nv)],
        inc=s,
    )
    st.last_sqrt[i] = sq2
    rv = ph.emit("vector", lambda e: e.reciprocal(inv_ap, std), waits=[(s, sq2)], inc=v)
    ph.emit("vector", lambda e: None, waits=[(v, rv)])
    return rv, sm


def ln_stats(ph, st, sems, eps_ap, x_ap, mean_ap, inv_ap, waits_v=(), waits_s=()):
    tok = ln_sums(ph, st, sems, x_ap, waits_v=waits_v, waits_s=waits_s)
    return ln_fin(ph, st, sems, eps_ap, tok, mean_ap, inv_ap)


def build_nc():
    nc = bass.Bass("TRN2", target_bir_lowering=False)

    xrows_h = nc.dram_tensor("xrows", [RPC, DM], F32, kind="ExternalInput").ap()
    xrep_h = nc.dram_tensor("xrep", [256, DM], F32, kind="ExternalInput").ap()
    blt_h = nc.dram_tensor("blt", [256, 256], F32, kind="ExternalInput").ap()
    clt_h = nc.dram_tensor("clt", [256, 256], F32, kind="ExternalInput").ap()
    alt_h = nc.dram_tensor("alt", [256, 256], F32, kind="ExternalInput").ap()
    rl_h = nc.dram_tensor("rl", [P, 1], F32, kind="ExternalInput").ap()
    flag_h = nc.dram_tensor("flag", [P, 1], F32, kind="ExternalInput").ap()
    ident_h = nc.dram_tensor("identb", [P, P], BF16, kind="ExternalInput").ap()
    zc_h = nc.dram_tensor("zc", [1, 1], BF16, kind="ExternalInput").ap()
    wg_h = nc.dram_tensor("wg", [DM, II], F32, kind="ExternalInput").ap()
    wu_h = nc.dram_tensor("wu", [DM, II], F32, kind="ExternalInput").ap()
    wd_h = nc.dram_tensor("wd", [II, DM], F32, kind="ExternalInput").ap()
    out_h = nc.dram_tensor("out", [RPC, DM], F32, kind="ExternalOutput").ap()

    xspill_h = nc.dram_tensor("xspill", [RPC, DM], F32, kind="Internal").ap()
    hscr_h = nc.dram_tensor("hscr", [NRT, P, II], BF16, kind="Internal").ap()
    x2t_h = nc.dram_tensor("x2t", [NRT, P, DM], BF16, kind="Internal").ap()
    ffnscr_h = nc.dram_tensor("ffnscr", [NRT, P, DM], F32, kind="Internal").ap()

    ctx0 = ExitStack()

    def sb(ctx, name, shape, dtype):
        return ctx.enter_context(nc.sbuf_tensor(name, shape, dtype))

    def psum(ctx, name, shape, dtype):
        return ctx.enter_context(nc.psum_tensor(name, shape, dtype))

    def mksem(name):
        return Sem(ctx0.enter_context(nc.semaphore(name)))

    sem_c = mksem("sem_c")
    sem_zc = mksem("sem_zc")
    sem_xa = [mksem(f"sem_xa{i}") for i in range(3)]
    sem_xb = [mksem(f"sem_xb{i}") for i in range(3)]
    sem_sp = [mksem(f"sem_sp{i}") for i in range(2)]
    sem_s2 = [mksem(f"sem_s2{i}") for i in range(2)]
    sem_x2r = mksem("sem_x2r")
    sem_hw = [mksem(f"sem_hw{i}") for i in range(2)]
    sem_hr = [mksem(f"sem_hr{i}") for i in range(2)]
    sem_fw = [mksem(f"sem_fw{i}") for i in range(4)]
    sem_f3 = [mksem(f"sem_f3{i}") for i in range(4)]
    sem_o = [mksem(f"sem_o{i}") for i in range(4)]
    sem_wg = [mksem(f"sem_wg{i}") for i in range(2)]
    sem_wd = [mksem(f"sem_wd{i}") for i in range(2)]
    pe = Sem(ctx0.enter_context(nc.semaphore("pe")))
    v = Sem(ctx0.enter_context(nc.semaphore("v")))
    s = Sem(ctx0.enter_context(nc.semaphore("s")))
    sems = {"pe": pe, "v": v, "s": s}

    ident_sb = sb(ctx0, "ident_sb", [P, P], BF16)
    rl_sb = sb(ctx0, "rl_sb", [P, 1], F32)
    flag_sb = sb(ctx0, "flag_sb", [P, 1], F32)
    rl1f_sb = sb(ctx0, "rl1f_sb", [P, 1], F32)
    f1m_sb = sb(ctx0, "f1m_sb", [P, 1], F32)
    eps_sb = sb(ctx0, "eps_sb", [P, 1], F32)
    st1 = sb(ctx0, "st1", [P, 2 * NRT], F32)
    strep = sb(ctx0, "strep", [P, 4], F32)
    stA = sb(ctx0, "stA", [P, 8], F32)
    lnsc = LNScratch(nc, ctx0)
    eps_ap = eps_sb[:, 0:1]

    ctx_ab = ExitStack()
    xln1t = sb(ctx_ab, "xln1t", [P, NRT * DM], BF16)
    xln1s = sb(ctx_ab, "xln1s", [P, NRT * DM], BF16)
    xrept = sb(ctx_ab, "xrept", [P, 2 * DM], BF16)
    alt_b = sb(ctx_ab, "alt_b", [P, 512], BF16)
    att_sb = sb(ctx_ab, "att_sb", [P, NSUB * DM], BF16)

    ctx_uw = ExitStack()
    U_sb = sb(ctx_uw, "U_sb", [P, 2 * DM], BF16)
    W_sb = sb(ctx_uw, "W_sb", [P, 2 * DM], BF16)

    ctx_a = ExitStack()
    NXA = 3
    xa = [sb(ctx_a, f"xa{i}", [P, DM], F32) for i in range(NXA)]
    xrep_sb = sb(ctx_a, "xrep_sb", [P, 2 * DM], F32)
    xrep_lnb = sb(ctx_a, "xrep_lnb", [P, 2 * DM], BF16)
    blt_b = sb(ctx_a, "blt_b", [P, 512], BF16)
    clt_b = sb(ctx_a, "clt_b", [P, 512], BF16)
    psu = [psum(ctx_a, f"psua{i}", [P, 512], F32) for i in range(2)]
    pst = [psum(ctx_a, f"psta{i}", [P, 1024], BF16) for i in range(2)]
    NB1 = 4
    pss = [psum(ctx_a, f"pss{i}", [P, 512], F32) for i in range(NB1)]

    ph = Ph(nc)
    ph.emit("sync", lambda e: e.dma_start(out=ident_sb[:, :], in_=ident_h), inc=sem_c, amt=16)
    ph.emit("sync", lambda e: e.dma_start(out=rl_sb[:, :], in_=rl_h), inc=sem_c, amt=16)
    ph.emit("sync", lambda e: e.dma_start(out=flag_sb[:, :], in_=flag_h), inc=sem_c, amt=16)
    ph.emit("gpsimd", lambda e: e.dma_start(
        out=blt_b[:, :].rearrange("p (c i) -> p c i", i=256),
        in_=blt_h.rearrange("(c p) i -> p c i", c=2)), inc=sem_c, amt=16)
    ph.emit("gpsimd", lambda e: e.dma_start(
        out=clt_b[:, :].rearrange("p (c i) -> p c i", i=256),
        in_=clt_h.rearrange("(c p) i -> p c i", c=2)), inc=sem_c, amt=16)
    ph.emit("gpsimd", lambda e: e.dma_start(
        out=alt_b[:, :].rearrange("p (c i) -> p c i", i=256),
        in_=alt_h.rearrange("(c p) i -> p c i", c=2)), inc=sem_c, amt=16)
    d_consts = ph.emit("sync", lambda e: e.dma_start(
        out=xrep_sb[:, :].rearrange("p (c n) -> p c n", c=2),
        in_=xrep_h.rearrange("(c p) n -> p c n", c=2)), inc=sem_c, amt=16)

    ph.emit("vector", lambda e: e.memset(eps_sb[:, :], EPS))
    fv = ph.emit("vector", lambda e: e.tensor_scalar(
        f1m_sb[:, :], flag_sb[:, :], -1.0, 1.0, ALU.mult, ALU.add),
        waits=[(sem_c, d_consts)], inc=v)
    ph.emit("vector", lambda e: e.tensor_mul(rl1f_sb[:, :], rl_sb[:, :], f1m_sb[:, :]),
            waits=[(v, fv)])

    s_cast = 0

    v_replb = [None, None]
    tok_rep = [None, None]
    for jc in range(2):
        x_ap = xrep_sb[:, jc * DM : (jc + 1) * DM]
        tok_rep[jc] = ln_sums(ph, lnsc, sems, x_ap,
                              waits_v=[(sem_c, d_consts)], waits_s=[(sem_c, d_consts)])

    xa_slot = [Slot() for _ in range(NXA)]
    tok_rt = [None] * NRT
    d_xa = [None] * NRT

    def emit_load_sums(rt):
        slot = xa_slot[rt % NXA]
        d_xa[rt] = ph.emit("sync", lambda e, rt=rt: e.dma_start(
            out=xa[rt % NXA][:, :], in_=xrows_h[rt * P : (rt + 1) * P, :]),
            waits=[slot.free] if slot.free else [], inc=sem_xa[rt % NXA], amt=16)
        tok_rt[rt] = ln_sums(ph, lnsc, sems, xa[rt % NXA][:, :],
                             waits_v=[(sem_xa[rt % NXA], d_xa[rt])],
                             waits_s=[(sem_xa[rt % NXA], d_xa[rt])])

    def emit_repl_fin(jc):
        mean_ap = strep[:, 2 * jc : 2 * jc + 1]
        inv_ap = strep[:, 2 * jc + 1 : 2 * jc + 2]
        x_ap = xrep_sb[:, jc * DM : (jc + 1) * DM]
        ln_fin(ph, lnsc, sems, eps_ap, tok_rep[jc], mean_ap, inv_ap)
        v_replb[jc] = ph.emit("vector", lambda e, x_ap=x_ap, m=mean_ap, iv=inv_ap, jc=jc: e.tensor_scalar(
            xrep_lnb[:, jc * DM : (jc + 1) * DM], x_ap, m, iv, ALU.subtract, ALU.mult),
            inc=v)

    emit_load_sums(0)
    emit_repl_fin(0)
    emit_load_sums(1)
    emit_repl_fin(1)

    pst_slot = [Slot(), Slot()]
    for jc in range(2):
        for half in range(2):
            bank = pst[half % 2]
            slot = pst_slot[half % 2]

            def tgrp(e, jc=jc, half=half, bank=bank):
                last = None
                for t in range(8):
                    ns = half * 8 + t
                    last = e.transpose(
                        bank[:, t * P : (t + 1) * P],
                        xrep_lnb[:, jc * DM + ns * P : jc * DM + (ns + 1) * P],
                        ident_sb[:, :],
                    )
                return last

            g = ph.emit("tensor", tgrp,
                        waits=[(v, v_replb[jc]), (sem_c, d_consts)] + ([slot.free] if slot.free else []),
                        inc=pe)
            sc = ph.emit("scalar", lambda e, jc=jc, half=half, bank=bank: e.copy(
                xrept[:, jc * DM + half * 1024 : jc * DM + (half + 1) * 1024], bank[:, :]),
                waits=[(pe, g)], inc=s)
            slot.free = (s, sc)

    psu_slot = [Slot(), Slot()]
    s_uw = {}
    for mi, (mat_b, dst) in enumerate(((blt_b, U_sb), (clt_b, W_sb))):
        for ic in range(2):
            for nb in range(4):
                k = mi * 8 + ic * 4 + nb
                bank = psu[k % 2]
                slot = psu_slot[k % 2]

                def mmg(e, mat_b=mat_b, ic=ic, nb=nb, bank=bank):
                    last = None
                    for jc in range(2):
                        last = e.matmul(
                            bank[:, :],
                            mat_b[:, jc * 256 + ic * P : jc * 256 + (ic + 1) * P],
                            xrep_lnb[:, jc * DM + nb * 512 : jc * DM + (nb + 1) * 512],
                            start=(jc == 0), stop=(jc == 1),
                        )
                    return last

                g = ph.emit("tensor", mmg,
                            waits=[(sem_c, d_consts), (v, v_replb[1])] + ([slot.free] if slot.free else []),
                            inc=pe)
                sc = ph.emit("scalar", lambda e, dst=dst, ic=ic, nb=nb, bank=bank: e.copy(
                    dst[:, ic * DM + nb * 512 : ic * DM + (nb + 1) * 512], bank[:, :]),
                    waits=[(pe, g)], inc=s)
                s_uw[(mi, ic, nb)] = sc
                slot.free = (s, sc)

    def emit_rt_post(rt):
        mean_ap = st1[:, 2 * rt : 2 * rt + 1]
        inv_ap = st1[:, 2 * rt + 1 : 2 * rt + 2]
        ln_fin(ph, lnsc, sems, eps_ap, tok_rt[rt], mean_ap, inv_ap)
        vm = ph.emit("vector", lambda e, rt=rt, m=mean_ap, iv=inv_ap: e.tensor_scalar(
            xln1s[:, rt * DM : (rt + 1) * DM], xa[rt % NXA][:, :], m, iv,
            ALU.subtract, ALU.mult), inc=v)
        xa_slot[rt % NXA].free = (v, vm)
        for half in range(2):
            bank = pst[half % 2]
            tslot = pst_slot[half % 2]

            def tgrp(e, rt=rt, half=half, bank=bank):
                last = None
                for t in range(8):
                    ns = half * 8 + t
                    last = e.transpose(
                        bank[:, t * P : (t + 1) * P],
                        xln1s[:, rt * DM + ns * P : rt * DM + (ns + 1) * P],
                        ident_sb[:, :],
                    )
                return last

            g = ph.emit("tensor", tgrp,
                        waits=[(v, vm)] + ([tslot.free] if tslot.free else []), inc=pe)
            sc = ph.emit("scalar", lambda e, rt=rt, half=half, bank=bank: e.copy(
                xln1t[:, rt * DM + half * 1024 : rt * DM + (half + 1) * 1024], bank[:, :]),
                waits=[(pe, g)], inc=s)
            tslot.free = (s, sc)

    emit_rt_post(0)
    for rt in range(2, NRT):
        emit_load_sums(rt)
        emit_rt_post(rt - 1)
    emit_rt_post(NRT - 1)

    pss_slot = [Slot() for _ in range(NB1)]
    s_relu_last = None
    v_relu_last = None
    for mb in range(NMB):
        for nt in range(NSUB):
            k = mb * NSUB + nt
            bank = pss[k % NB1]
            slot = pss_slot[k % NB1]
            uw_wait = max(s_uw[(0, 0, nt // 4)], s_uw[(0, 1, nt // 4)],
                          s_uw[(1, 0, mb)], s_uw[(1, 1, mb)])

            def mmg(e, nt=nt, mb=mb, bank=bank):
                last = None
                for dc in range(2):
                    last = e.matmul(
                        bank[:, :],
                        U_sb[:, dc * DM + nt * P : dc * DM + (nt + 1) * P],
                        W_sb[:, dc * DM + mb * MBS : dc * DM + (mb + 1) * MBS],
                        start=(dc == 0), stop=(dc == 1),
                    )
                return last

            g = ph.emit("tensor", mmg,
                        waits=[(s, uw_wait)] + ([slot.free] if slot.free else []),
                        inc=pe)
            dst = att_sb[:, nt * DM + mb * MBS : nt * DM + (mb + 1) * MBS]
            if k % 2 == 0:
                sc = ph.emit("scalar", lambda e, dst=dst, bank=bank: e.activation(
                    dst, bank[:, :], ACT.Relu), waits=[(pe, g)], inc=s)
                slot.free = (s, sc)
                s_relu_last = sc
            else:
                vc = ph.emit("vector", lambda e, dst=dst, bank=bank: e.tensor_relu(
                    dst, bank[:, :]), waits=[(pe, g)], inc=v)
                slot.free = (v, vc)
                v_relu_last = vc
    dz = ph.emit("sync", lambda e: e.dma_start(
        out=att_sb[127:128, 15 * DM + 2047 : 15 * DM + 2048], in_=zc_h[0:1, 0:1]),
        waits=[(s, s_relu_last), (v, v_relu_last)], inc=sem_zc, amt=16)
    ph.emit("sync", lambda e: None, waits=[(sem_zc, dz)])
    ph.run()
    ctx_a.close()
    ctx_uw.close()

    ctx_b2 = ExitStack()
    NXB = 3
    xb = [sb(ctx_b2, f"xb{i}", [P, DM], F32) for i in range(NXB)]
    amid = [sb(ctx_b2, f"amid{i}", [P, DM], F32) for i in range(2)]
    xatt_b = sb(ctx_b2, "xatt_b", [P, 2 * DM], BF16)
    xln2s = [sb(ctx_b2, f"xln2s{i}", [P, DM], BF16) for i in range(2)]
    xln2s_slot = [Slot(), Slot()]
    stage2 = [sb(ctx_b2, f"stage2{i}", [P, 1024], BF16) for i in range(2)]
    st2 = sb(ctx_b2, "st2", [P, 8], F32)
    psy = [psum(ctx_b2, f"psy{i}", [P, 512], F32) for i in range(3)]
    pst2 = [psum(ctx_b2, f"pst2{i}", [P, 1024], BF16) for i in range(2)]

    ph = Ph(nc)
    psy_slot = [Slot(), Slot(), Slot()]
    pst2_slot = [Slot(), Slot()]
    xb_slot = [Slot() for _ in range(NXB)]
    psy_i = 0
    d_xb = [None] * (NRT + 4)

    def load_xb(rt):
        slot = xb_slot[rt % NXB]
        d_xb[rt] = ph.emit("sync", lambda e, rt=rt: e.dma_start(
            out=xb[rt % NXB][:, :], in_=xrows_h[rt * P : (rt + 1) * P, :]),
            waits=[slot.free] if slot.free else [], inc=sem_xb[rt % NXB], amt=16)

    for _rt in range(NXB):
        load_xb(_rt)

    v_amid = [[None] * NMB for _ in range(NRT)]
    v_amid_fin = [None] * NRT
    s_xatt = [[None] * NMB for _ in range(2)]
    stage2_slots = [Slot(), Slot()]

    def emit_y(rt):
        nonlocal psy_i
        rl_scal = rl1f_sb if rt < 2 else rl_sb
        for mb in range(NMB):
            bank = psy[psy_i % 3]
            slot = psy_slot[psy_i % 3]
            psy_i += 1

            def mmg(e, rt=rt, mb=mb, bank=bank):
                last = None
                for ns in range(NSUB):
                    last = e.matmul(
                        bank[:, :],
                        xln1t[:, rt * DM + ns * P : rt * DM + (ns + 1) * P],
                        att_sb[:, ns * DM + mb * MBS : ns * DM + (mb + 1) * MBS],
                        start=(ns == 0), stop=(ns == NSUB - 1),
                    )
                return last

            g = ph.emit("tensor", mmg, waits=[slot.free] if slot.free else [], inc=pe)
            aw = [(pe, g)]
            if rt >= 2 and tokA[rt - 2] is not None:
                aw.append((s, tokA[rt - 2][1]))
            vm = ph.emit("vector", lambda e, rt=rt, mb=mb, bank=bank, rl_scal=rl_scal: e.scalar_tensor_tensor(
                amid[rt % 2][:, mb * MBS : (mb + 1) * MBS], bank[:, :], rl_scal[:, 0:1],
                xln1s[:, rt * DM + mb * MBS : rt * DM + (mb + 1) * MBS],
                ALU.mult, ALU.add),
                waits=aw, inc=v)
            v_amid[rt][mb] = vm
            v_amid_fin[rt] = vm
            slot.free = (v, vm)

    def emit_xatt(jc):
        nonlocal psy_i
        for mb in range(NMB):
            bank = psy[psy_i % 3]
            slot = psy_slot[psy_i % 3]
            psy_i += 1

            def mmg(e, jc=jc, mb=mb, bank=bank):
                last = None
                for ns in range(NSUB):
                    last = e.matmul(
                        bank[:, :],
                        xrept[:, jc * DM + ns * P : jc * DM + (ns + 1) * P],
                        att_sb[:, ns * DM + mb * MBS : ns * DM + (mb + 1) * MBS],
                        start=(ns == 0), stop=(ns == NSUB - 1),
                    )
                return last

            g = ph.emit("tensor", mmg, waits=[slot.free] if slot.free else [], inc=pe)
            sc = ph.emit("scalar", lambda e, jc=jc, mb=mb, bank=bank: e.copy(
                xatt_b[:, jc * DM + mb * MBS : jc * DM + (mb + 1) * MBS], bank[:, :]),
                waits=[(pe, g)], inc=s)
            s_xatt[jc][mb] = sc
            slot.free = (s, sc)

    def emit_al():
        nonlocal psy_i
        for ic in range(2):
            for mb in range(NMB):
                bank = psy[psy_i % 3]
                slot = psy_slot[psy_i % 3]
                psy_i += 1

                def mmg(e, ic=ic, mb=mb, bank=bank):
                    last = None
                    for jc in range(2):
                        last = e.matmul(
                            bank[:, :],
                            alt_b[:, jc * 256 + ic * P : jc * 256 + (ic + 1) * P],
                            xatt_b[:, jc * DM + mb * MBS : jc * DM + (mb + 1) * MBS],
                            start=(jc == 0), stop=(jc == 1),
                        )
                    return last

                g = ph.emit("tensor", mmg,
                            waits=[(s, s_xatt[1][mb]), (sem_c, d_consts)] + ([slot.free] if slot.free else []),
                            inc=pe)
                vm = ph.emit("vector", lambda e, ic=ic, mb=mb, bank=bank: e.scalar_tensor_tensor(
                    amid[ic][:, mb * MBS : (mb + 1) * MBS], bank[:, :], flag_sb[:, 0:1],
                    amid[ic][:, mb * MBS : (mb + 1) * MBS], ALU.mult, ALU.add),
                    waits=[(pe, g)], inc=v)
                v_amid_fin[ic] = vm
                slot.free = (v, vm)

    tokA = [None] * NRT
    tok2 = [None] * NRT

    def emit_p1(rt):
        tokA[rt] = ln_sums(ph, lnsc, sems, amid[rt % 2][:, :],
                           waits_s=[(v, v_amid_fin[rt])])

    def emit_p2(rt):
        a_ap = amid[rt % 2][:, :]
        sA = stA[:, 2 * (rt % 4) : 2 * (rt % 4) + 1]
        sAi = stA[:, 2 * (rt % 4) + 1 : 2 * (rt % 4) + 2]
        ln_fin(ph, lnsc, sems, eps_ap, tokA[rt], sA, sAi)
        vn = ph.emit("vector", lambda e, a_ap=a_ap, sA=sA, sAi=sAi: e.tensor_scalar(
            a_ap, a_ap, sA, sAi, ALU.subtract, ALU.mult), inc=v)
        vx = ph.emit("vector", lambda e, rt=rt, a_ap=a_ap: e.tensor_add(
            xb[rt % NXB][:, :], xb[rt % NXB][:, :], a_ap),
            waits=[(v, vn), (sem_xb[rt % NXB], d_xb[rt])], inc=v)
        tok2[rt] = ln_sums(ph, lnsc, sems, xb[rt % NXB][:, :], waits_s=[(v, vx)])

    def emit_p3(rt):
        s2m = st2[:, 2 * (rt % 4) : 2 * (rt % 4) + 1]
        s2i = st2[:, 2 * (rt % 4) + 1 : 2 * (rt % 4) + 2]
        _, sq2 = ln_fin(ph, lnsc, sems, eps_ap, tok2[rt], s2m, s2i)
        sq2 = tok2[rt][1]
        xsl = xln2s_slot[rt % 2]
        vm = ph.emit("vector", lambda e, rt=rt, s2m=s2m, s2i=s2i: e.tensor_scalar(
            xln2s[rt % 2][:, :], xb[rt % NXB][:, :], s2m, s2i,
            ALU.subtract, ALU.mult),
            waits=[xsl.free] if xsl.free else [], inc=v)
        dsp = ph.emit("gpsimd", lambda e, rt=rt: e.dma_start(
            out=xspill_h[rt * P : (rt + 1) * P, :], in_=xb[rt % NXB][:, :]),
            waits=[(v, vm), (s, sq2)], inc=sem_sp[rt % 2], amt=16)
        xb_slot[rt % NXB].free = (sem_sp[rt % 2], dsp)
        if rt + NXB < NRT:
            load_xb(rt + NXB)
        for half in range(2):
            bank = pst2[half % 2]
            tslot = pst2_slot[half % 2]

            def tgrp(e, rt=rt, half=half, bank=bank):
                last = None
                for t in range(8):
                    ns = half * 8 + t
                    last = e.transpose(
                        bank[:, t * P : (t + 1) * P],
                        xln2s[rt % 2][:, ns * P : (ns + 1) * P],
                        ident_sb[:, :],
                    )
                return last

            g = ph.emit("tensor", tgrp,
                        waits=[(v, vm)] + ([tslot.free] if tslot.free else []), inc=pe)
            xsl.free = (pe, g)
            stg_slot = stage2[half % 2]
            sslot = stage2_slots[half % 2]
            sc = ph.emit("scalar", lambda e, half=half, bank=bank, stg=stg_slot: e.copy(
                stg[:, :], bank[:, :]),
                waits=[(pe, g)] + ([sslot.free] if sslot.free else []), inc=s)
            tslot.free = (s, sc)
            dd = ph.emit("gpsimd", lambda e, rt=rt, half=half, stg=stg_slot: e.dma_start(
                out=x2t_h[rt][:, half * 1024 : (half + 1) * 1024], in_=stg[:, :]),
                waits=[(s, sc)], inc=sem_s2[half % 2], amt=16)
            sslot.free = (sem_s2[half % 2], dd)

    emit_y(0)
    emit_xatt(0)
    emit_y(1)
    emit_xatt(1)
    emit_al()
    emit_p1(0)
    emit_p1(1)
    emit_p2(0)
    emit_y(2)
    emit_p1(2)
    for k in range(3, NRT):
        emit_p3(k - 3)
        emit_p2(k - 2)
        emit_y(k)
        emit_p1(k)
    emit_p2(NRT - 2)
    emit_p3(NRT - 3)
    emit_p2(NRT - 1)
    emit_p3(NRT - 2)
    emit_p3(NRT - 1)
    ph.emit("sync", lambda e: None,
            waits=[(sem_sp[0], sem_sp[0].n), (sem_sp[1], sem_sp[1].n),
                   (sem_s2[0], sem_s2[0].n), (sem_s2[1], sem_s2[1].n)])
    ph.run()
    ctx_b2.close()
    ctx_ab.close()

    ctx_f12 = ExitStack()
    wdbuf0 = sb(ctx_f12, "wdbuf0", [P, ISUBS * 512], BF16)
    ctx_f1 = ExitStack()
    xln2t = sb(ctx_f1, "xln2t", [P, NRT * DM], BF16)
    wgbuf = [sb(ctx_f1, f"wgbuf{i}", [P, NSUB * 512], BF16) for i in range(2)]
    wubuf = [sb(ctx_f1, f"wubuf{i}", [P, NSUB * 512], BF16) for i in range(2)]
    sbg = [sb(ctx_f1, f"sbg{i}", [P, ICS], F32) for i in range(2)]
    sbh = [sb(ctx_f1, f"sbh{i}", [P, ICS], F32) for i in range(2)]
    sba = [sb(ctx_f1, f"sba{i}", [P, ICS], F32) for i in range(2)]
    maskb = sb(ctx_f1, "maskb", [P, ICS], F32)
    Mb = sb(ctx_f1, "Mb", [P, ICS // 2], F32)
    mb_ = sb(ctx_f1, "mb_", [P, ICS // 2], F32)
    mmb = sb(ctx_f1, "mmb", [P, ICS // 4], F32)
    mxb = sb(ctx_f1, "mxb", [P, ICS // 4], F32)
    t2b = sb(ctx_f1, "t2b", [P, ICS // 4], F32)
    hsp = [sb(ctx_f1, f"hsp{i}", [P, ICS], BF16) for i in range(2)]
    hst = [sb(ctx_f1, f"hst{i}", [P, ICS], BF16) for i in range(2)]
    psg = [psum(ctx_f1, f"psg{i}", [P, 512], F32) for i in range(2)]
    psup = [psum(ctx_f1, f"psup{i}", [P, 512], F32) for i in range(2)]
    pstt = [psum(ctx_f1, f"pstt{i}", [P, 512], BF16) for i in range(2)]

    ph = Ph(nc)
    d_x2t_all = ph.emit("sync", lambda e: e.dma_start(
        out=xln2t[:, :].rearrange("p (r f) -> p r f", f=DM),
        in_=x2t_h.rearrange("r p f -> p r f")),
        waits=[(sem_s2[0], sem_s2[0].n), (sem_s2[1], sem_s2[1].n)],
        inc=sem_x2r, amt=16)

    NK = NIC * NRT
    snap_w = [None] * NIC
    peG = [None] * NK
    peU = [None] * NK
    peT = [None] * NK
    s_silu = [None] * NK
    s_abs = [None] * NK
    s_hst = [None] * NK
    v_h = [None] * NK
    v_hsp = [None] * NK
    d_hst = [None] * NK

    def emit_transposes(k):
        bank = pstt[k % 2]
        w = [(v, v_hsp[k])]
        if k - 2 >= 0:
            w.append((s, s_hst[k - 2]))

        def tgrp(e, k=k, bank=bank):
            last = None
            for q in range(NQ):
                last = e.transpose(
                    bank[:, q * P : (q + 1) * P],
                    hsp[k % 2][:, q * P : (q + 1) * P],
                    ident_sb[:, :],
                )
            return last

        peT[k] = ph.emit("tensor", tgrp, waits=w, inc=pe)

    def emit_hstage(k):
        wk = [(pe, peT[k])]
        if k - 2 >= 0:
            wk.append((sem_hw[k % 2], d_hst[k - 2]))
        s_hst[k] = ph.emit("scalar", lambda e, k=k: e.copy(hst[k % 2][:, :], pstt[k % 2][:, :]),
                           waits=wk, inc=s)

    def emit_hdma(k):
        ic, rt = divmod(k, NRT)
        d_hst[k] = ph.emit("sync", lambda e, ic=ic, rt=rt, k=k: e.dma_start(
            out=hscr_h[rt][:, ic * ICS : (ic + 1) * ICS], in_=hst[k % 2][:, :]),
            waits=[(s, s_hst[k])], inc=sem_hw[k % 2], amt=16)

    for ic in range(NIC):
        wslot_wait = []
        if ic - 2 >= 0:
            wslot_wait = [(pe, peU[(ic - 2) * NRT + NRT - 1])]
        ph.emit("gpsimd", lambda e, ic=ic: e.dma_start(
            out=wgbuf[ic % 2][:, :].rearrange("p (c f) -> p c f", f=512),
            in_=wg_h[:, ic * ICS : (ic + 1) * ICS].rearrange("(c p) f -> p c f", p=P)),
            waits=wslot_wait, inc=sem_wg[ic % 2], amt=16)
        snap_w[ic] = ph.emit("gpsimd", lambda e, ic=ic: e.dma_start(
            out=wubuf[ic % 2][:, :].rearrange("p (c f) -> p c f", f=512),
            in_=wu_h[:, ic * ICS : (ic + 1) * ICS].rearrange("(c p) f -> p c f", p=P)),
            inc=sem_wg[ic % 2], amt=16)

        for rt in range(NRT):
            k = ic * NRT + rt
            wg_waits = [(sem_wg[ic % 2], snap_w[ic])]
            if k == 0:
                wg_waits.append((sem_x2r, d_x2t_all))
            if k - 2 >= 0:
                wg_waits.append((s, s_silu[k - 2]))

            def gate(e, ic=ic, rt=rt, k=k):
                last = None
                for ns in range(NSUB):
                    last = e.matmul(
                        psg[k % 2][:, :],
                        xln2t[:, rt * DM + ns * P : rt * DM + (ns + 1) * P],
                        wgbuf[ic % 2][:, ns * 512 : (ns + 1) * 512],
                        start=(ns == 0), stop=(ns == NSUB - 1),
                    )
                return last

            peG[k] = ph.emit("tensor", gate, waits=wg_waits, inc=pe)

            wu_waits = []
            if k - 2 >= 0:
                wu_waits.append((v, v_h[k - 2]))

            def up(e, ic=ic, rt=rt, k=k):
                last = None
                for ns in range(NSUB):
                    last = e.matmul(
                        psup[k % 2][:, :],
                        xln2t[:, rt * DM + ns * P : rt * DM + (ns + 1) * P],
                        wubuf[ic % 2][:, ns * 512 : (ns + 1) * 512],
                        start=(ns == 0), stop=(ns == NSUB - 1),
                    )
                return last

            peU[k] = ph.emit("tensor", up, waits=wu_waits, inc=pe)
            if k - 1 >= 0:
                emit_transposes(k - 1)

            sw = [(pe, peG[k])]
            if k - 2 >= 0:
                sw.append((v, v_h[k - 2]))
            s_silu[k] = ph.emit("scalar", lambda e, k=k: e.activation(
                sbg[k % 2][:, :], psg[k % 2][:, :], ACT.Silu), waits=sw, inc=s)
            v_h[k] = ph.emit("vector", lambda e, k=k: e.tensor_mul(
                sbh[k % 2][:, :], sbg[k % 2][:, :], psup[k % 2][:, :]),
                waits=[(s, s_silu[k]), (pe, peU[k])], inc=v)
            s_abs[k] = ph.emit("scalar", lambda e, k=k: e.activation(
                sba[k % 2][:, :], sbh[k % 2][:, :], ACT.Abs), waits=[(v, v_h[k])], inc=s)
            if k - 1 >= 0:
                emit_hstage(k - 1)

            a = sba[k % 2]
            spw = [(s, s_abs[k])]
            if k - 2 >= 0:
                spw.append((pe, peT[k - 2]))
            vM = ph.emit("vector", lambda e, a=a: e.tensor_tensor(
                Mb[:, :], a[:, 0:ICS:2], a[:, 1:ICS:2], ALU.max), waits=spw, inc=v)
            vm_ = ph.emit("vector", lambda e, a=a: e.tensor_tensor(
                mb_[:, :], a[:, 0:ICS:2], a[:, 1:ICS:2], ALU.min), inc=v)
            ph.emit("vector", lambda e: e.tensor_tensor(
                mmb[:, :], Mb[:, 0 : ICS // 2 : 2], Mb[:, 1 : ICS // 2 : 2], ALU.min),
                waits=[(v, vM)])
            vmx = ph.emit("vector", lambda e: e.tensor_tensor(
                mxb[:, :], mb_[:, 0 : ICS // 2 : 2], mb_[:, 1 : ICS // 2 : 2], ALU.max),
                waits=[(v, vm_)], inc=v)
            vt2 = ph.emit("vector", lambda e: e.tensor_tensor(
                t2b[:, :], mmb[:, :], mxb[:, :], ALU.max), waits=[(v, vmx)], inc=v)
            vge = ph.emit("vector", lambda e, a=a: e.tensor_tensor(
                maskb[:, :].rearrange("p (g q) -> p g q", q=4),
                a[:, :].rearrange("p (g q) -> p g q", q=4),
                t2b[:, :].unsqueeze(2).broadcast_to([P, ICS // 4, 4]),
                ALU.is_ge), waits=[(v, vt2)], inc=v)
            v_hsp[k] = ph.emit("vector", lambda e, k=k: e.tensor_mul(
                hsp[k % 2][:, :], sbh[k % 2][:, :], maskb[:, :]),
                waits=[(v, vge)], inc=v)
            if k - 1 >= 0:
                emit_hdma(k - 1)

    k = NK - 1
    emit_transposes(k)
    emit_hstage(k)
    emit_hdma(k)
    snap_wd_pre = None
    for t in range(4):
        snap_wd_pre = ph.emit("gpsimd", lambda e, t=t: e.dma_start(
            out=wdbuf0[:, t * 16 * 512 : (t + 1) * 16 * 512].rearrange(
                "p (c f) -> p c f", f=512),
            in_=wd_h[t * 2048 : (t + 1) * 2048, 0:OBS].rearrange(
                "(c p) f -> p c f", p=P)),
            inc=sem_wd[0], amt=16)
    ph.emit("sync", lambda e: None,
            waits=[(sem_hw[0], sem_hw[0].n), (sem_hw[1], sem_hw[1].n)])
    ph.run()
    ctx_f1.close()

    ctx_f2 = ExitStack()
    wdbuf = [wdbuf0, sb(ctx_f2, "wdbuf1", [P, ISUBS * 512], BF16)]
    hrt = [sb(ctx_f2, f"hrt{i}", [P, II], BF16) for i in range(2)]
    ostage = [sb(ctx_f2, f"ostage{i}", [P, OBS], F32) for i in range(4)]
    psd = [psum(ctx_f2, f"psd{i}", [P, 512], F32) for i in range(4)]
    xt = [sb(ctx_f2, f"xt{i}", [P, DM], F32) for i in range(2)]
    ft = [sb(ctx_f2, f"ft{i}", [P, DM], F32) for i in range(2)]
    stC = sb(ctx_f2, "stC", [P, 8], F32)

    ph = Ph(nc)
    v_out = [None] * NRT
    d_out = [None] * NRT
    sq_f3 = [None] * NRT
    tokC = [None] * NRT
    d_ft = [None] * NRT

    def f3_load(rt, fw_wait):
        w = [fw_wait]
        if rt == 0:
            w += [(sem_sp[0], sem_sp[0].n), (sem_sp[1], sem_sp[1].n)]
        if rt - 2 >= 0:
            w += [(v, v_out[rt - 2]), (s, sq_f3[rt - 2]),
                  (sem_o[(rt - 2) % 4], d_out[rt - 2])]
        ph.emit("gpsimd", lambda e, rt=rt: e.dma_start(
            out=xt[rt % 2][:, :], in_=xspill_h[rt * P : (rt + 1) * P, :]),
            waits=w, inc=sem_f3[rt % 2], amt=16)
        d_ft[rt] = ph.emit("gpsimd", lambda e, rt=rt: e.dma_start(
            out=ft[rt % 2][:, :], in_=ffnscr_h[rt]), inc=sem_f3[rt % 2], amt=16)

    def f3_sums(rt):
        tokC[rt] = ln_sums(ph, lnsc, sems, ft[rt % 2][:, :],
                           waits_v=[(sem_f3[rt % 2], d_ft[rt])],
                           waits_s=[(sem_f3[rt % 2], d_ft[rt])])
        sq_f3[rt] = tokC[rt][1]

    def f3_fin(rt):
        sCm = stC[:, 2 * (rt % 4) : 2 * (rt % 4) + 1]
        sCi = stC[:, 2 * (rt % 4) + 1 : 2 * (rt % 4) + 2]
        ln_fin(ph, lnsc, sems, eps_ap, tokC[rt], sCm, sCi)
        vn3 = ph.emit("vector", lambda e, rt=rt, sCm=sCm, sCi=sCi: e.tensor_scalar(
            ft[rt % 2][:, :], ft[rt % 2][:, :], sCm, sCi,
            ALU.subtract, ALU.mult), inc=v)
        v_out[rt] = ph.emit("vector", lambda e, rt=rt: e.tensor_add(
            ft[rt % 2][:, :], ft[rt % 2][:, :], xt[rt % 2][:, :]),
            waits=[(v, vn3)], inc=v)
        d_out[rt] = ph.emit("scalar", lambda e, rt=rt: e.dma_start(
            out=out_h[rt * P : (rt + 1) * P, :], in_=ft[rt % 2][:, :]),
            waits=[(v, v_out[rt])], inc=sem_o[rt % 4], amt=16)
    NJ = NOB * NRT
    peD = [None] * NJ
    v_oev = [None] * NJ
    d_hrt = [None] * NJ
    d_ost = [None] * NJ
    snap_wd = [None] * NOB

    snap_wd[0] = snap_wd_pre
    for ob in range(NOB):
        if ob > 0:
            wsl = []
            if ob - 2 >= 0:
                wsl = [(pe, peD[(ob - 2) * NRT + NRT - 1])]
            for t in range(4):
                snap_wd[ob] = ph.emit("gpsimd", lambda e, ob=ob, t=t: e.dma_start(
                    out=wdbuf[ob % 2][:, t * 16 * 512 : (t + 1) * 16 * 512].rearrange(
                        "p (c f) -> p c f", f=512),
                    in_=wd_h[t * 2048 : (t + 1) * 2048, ob * OBS : (ob + 1) * OBS].rearrange(
                        "(c p) f -> p c f", p=P)),
                    waits=wsl if t == 0 else [], inc=sem_wd[ob % 2], amt=16)

        for rt in range(NRT):
            j = ob * NRT + rt
            hw = []
            if j - 2 >= 0:
                hw = [(pe, peD[j - 2])]
            hw0 = [(sem_hw[0], sem_hw[0].n), (sem_hw[1], sem_hw[1].n)] if j == 0 else []
            d_hrt[j] = ph.emit("sync", lambda e, rt=rt, j=j: e.dma_start(
                out=hrt[j % 2][:, :], in_=hscr_h[rt]), waits=hw + hw0,
                inc=sem_hr[j % 2], amt=16)

            mw = [(sem_wd[ob % 2], snap_wd[ob]), (sem_hr[j % 2], d_hrt[j])]
            if j - 4 >= 0:
                mw.append((v, v_oev[j - 4]))

            def down(e, ob=ob, j=j):
                last = None
                for isub in range(ISUBS):
                    last = e.matmul(
                        psd[j % 4][:, :],
                        hrt[j % 2][:, isub * P : (isub + 1) * P],
                        wdbuf[ob % 2][:, isub * 512 : (isub + 1) * 512],
                        start=(isub == 0), stop=(isub == ISUBS - 1),
                    )
                return last

            peD[j] = ph.emit("tensor", down, waits=mw, inc=pe)
            ow = [(pe, peD[j])]
            if j - 4 >= 0:
                ow.append((sem_fw[j % 4], d_ost[j - 4]))
            v_oev[j] = ph.emit("vector", lambda e, j=j: e.tensor_copy(
                ostage[j % 4][:, :], psd[j % 4][:, :]), waits=ow, inc=v)
            d_ost[j] = ph.emit("scalar", lambda e, ob=ob, rt=rt, j=j: e.dma_start(
                out=ffnscr_h[rt][:, ob * OBS : (ob + 1) * OBS], in_=ostage[j % 4][:, :]),
                waits=[(v, v_oev[j])], inc=sem_fw[j % 4], amt=16)
            if ob == NOB - 1:
                if rt >= 1:
                    f3_sums(rt - 1)
                if rt >= 2:
                    f3_fin(rt - 2)
                f3_load(rt, (sem_fw[j % 4], d_ost[j]))

    f3_sums(NRT - 1)
    f3_fin(NRT - 2)
    f3_fin(NRT - 1)
    ph.emit("sync", lambda e: None,
            waits=[(so, so.n) for so in sem_o])
    ph.run()
    ctx_f2.close()
    ctx_f12.close()

    ctx0.close()
    return nc


def kernel(L0_attn_Z_in, L0_attn_A_l, L0_attn_r_l, L0_attn_B_l, L0_attn_C_l,
           L0_ffn_w_gate, L0_ffn_w_up, L0_ffn_w_down):
    Z = np.ascontiguousarray(np.asarray(L0_attn_Z_in, dtype=np.float32).reshape(2 * 4096, DM))
    A_l = np.asarray(L0_attn_A_l, dtype=np.float32)
    B_l = np.asarray(L0_attn_B_l, dtype=np.float32)
    C_l = np.asarray(L0_attn_C_l, dtype=np.float32)
    r_l = np.asarray(L0_attn_r_l, dtype=np.float32)
    wg = np.ascontiguousarray(np.asarray(L0_ffn_w_gate, dtype=np.float32))
    wu = np.ascontiguousarray(np.asarray(L0_ffn_w_up, dtype=np.float32))
    wd = np.ascontiguousarray(np.asarray(L0_ffn_w_down, dtype=np.float32))

    blt = np.ascontiguousarray(B_l.T)
    clt = np.ascontiguousarray(C_l.T)
    alt = np.ascontiguousarray(A_l.T)
    rl_rep = np.ascontiguousarray(np.broadcast_to(r_l.reshape(1, 1), (P, 1)).astype(np.float32))
    ident = np.eye(P, dtype=ml_dtypes.bfloat16)
    zc = np.zeros((1, 1), dtype=ml_dtypes.bfloat16)

    nc = build_nc()
    in_maps = []
    for c in range(NCORES):
        b = (c * RPC) // 4096
        in_maps.append({
            "xrows": np.ascontiguousarray(Z[c * RPC : (c + 1) * RPC]),
            "xrep": np.ascontiguousarray(Z[b * 4096 : b * 4096 + 256]),
            "blt": blt, "clt": clt, "alt": alt,
            "rl": rl_rep,
            "flag": np.full((P, 1), 1.0 if c % 4 == 0 else 0.0, dtype=np.float32),
            "identb": ident, "zc": zc,
            "wg": wg, "wu": wu, "wd": wd,
        })
    res = run_bass_kernel_spmd(nc, in_maps, core_ids=list(range(NCORES)))
    outs = [np.asarray(r["out"]) for r in res.results]
    return np.concatenate(outs, axis=0).reshape(2, 4096, DM).astype(np.float32)


# version 17
# speedup vs baseline: 1.0066x; 1.0054x over previous
import numpy as np
import ml_dtypes
from contextlib import ExitStack

import concourse.bass as bass
import concourse.mybir as mybir
from concourse.bass_utils import run_bass_kernel_spmd

F32 = mybir.dt.float32
BF16 = mybir.dt.bfloat16
ALU = mybir.AluOpType
ACT = mybir.ActivationFunctionType
AX = mybir.AxisListType

P = 128
DM = 2048
NSUB = DM // P
RPC = 1024
NRT = RPC // P
MBS = 512
NMB = DM // MBS
II = 8192
ICS = 512
NIC = II // ICS
NQ = ICS // P
OBS = 512
NOB = DM // OBS
ISUBS = II // P
NCORES = 8
EPS = 1e-5

ENGS = ("sync", "gpsimd", "scalar", "vector", "tensor")


class Sem:
    def __init__(self, h):
        self.h = h
        self.n = 0


class Slot:
    def __init__(self):
        self.free = None


class Ph:
    def __init__(self, nc):
        self.nc = nc
        self.q = {e: [] for e in ENGS}

    def emit(self, eng, fn, waits=(), inc=None, amt=1):
        ws = [(sm, val) for (sm, val) in waits if sm is not None and val > 0]
        self.q[eng].append((ws, fn, inc, amt))
        if inc is not None:
            inc.n += amt
            return inc.n
        return None

    def run(self):
        with self.nc.Block() as block:
            for e in ENGS:
                items = self.q[e]
                if not items:
                    continue

                def body(eng, items=items):
                    for ws, fn, inc, amt in items:
                        for sm, val in ws:
                            eng.wait_ge(sm.h, val)
                        inst = fn(eng)
                        if inc is not None:
                            inst.then_inc(inc.h, amt)

                getattr(block, e)(body)


class LNScratch:
    NSET = 4

    def __init__(self, nc, ctx):
        t = ctx.enter_context(nc.sbuf_tensor("ln_scr", [P, 8 * self.NSET], F32))
        self.ssum = [t[:, 8 * i + 0 : 8 * i + 1] for i in range(self.NSET)]
        self.ssq = [t[:, 8 * i + 1 : 8 * i + 2] for i in range(self.NSET)]
        self.ex2 = [t[:, 8 * i + 2 : 8 * i + 3] for i in range(self.NSET)]
        self.var = [t[:, 8 * i + 4 : 8 * i + 5] for i in range(self.NSET)]
        self.std = [t[:, 8 * i + 5 : 8 * i + 6] for i in range(self.NSET)]
        ds = [ctx.enter_context(nc.sbuf_tensor(f"ln_dump{i}", [P, DM], mybir.dt.float8e4))
              for i in range(2)]
        self.dumps = [x[:, :] for x in ds]
        self.i = 0
        self.last_ex2 = [0] * self.NSET
        self.last_mean = [0] * self.NSET
        self.last_sqrt = [0] * self.NSET


def ln_sums(ph, st, sems, x_ap, waits_v=(), waits_s=()):
    v, s = sems["v"], sems["s"]
    i = st.i % st.NSET
    dump = st.dumps[st.i % 2]
    dump_b = st.dumps[(st.i + 1) % 2]
    st.i += 1
    ph.emit(
        "scalar",
        lambda e: e.activation(dump, x_ap, ACT.Square, accum_out=st.ssq[i]),
        waits=list(waits_s) + [(v, st.last_ex2[i])],
    )
    ph.emit(
        "scalar",
        lambda e: e.activation(dump_b, x_ap, ACT.Copy, accum_out=st.ssum[i]),
        waits=list(waits_v) + [(v, st.last_mean[i])],
    )
    sm = ph.emit("scalar", lambda e: e.drain(), inc=s)
    return (i, sm)


def ln_fin(ph, st, sems, eps_ap, token, mean_ap, inv_ap):
    v, s = sems["v"], sems["s"]
    i, sm = token
    ssum, ssq, ex2, var, std = st.ssum[i], st.ssq[i], st.ex2[i], st.var[i], st.std[i]
    mn = ph.emit(
        "vector",
        lambda e: e.tensor_scalar(mean_ap, ssum, 1.0 / DM, None, ALU.mult, ALU.bypass),
        waits=[(s, sm)],
        inc=v,
    )
    st.last_mean[i] = mn
    ex = ph.emit(
        "vector",
        lambda e: e.tensor_scalar(ex2, ssq, 1.0 / DM, None, ALU.mult, ALU.bypass),
        inc=v,
    )
    st.last_ex2[i] = ex
    ph.emit("vector", lambda e: None, waits=[(v, ex)])
    nv = ph.emit(
        "vector",
        lambda e: e.scalar_tensor_tensor(var, mean_ap, mean_ap, ex2,
                                         ALU.mult, ALU.subtract),
        waits=[(s, st.last_sqrt[i])],
        inc=v,
    )
    sq2 = ph.emit(
        "scalar",
        lambda e: e.activation(std, var, ACT.Sqrt, scale=-1.0, bias=eps_ap),
        waits=[(v, nv)],
        inc=s,
    )
    st.last_sqrt[i] = sq2
    rv = ph.emit("vector", lambda e: e.reciprocal(inv_ap, std), waits=[(s, sq2)], inc=v)
    ph.emit("vector", lambda e: None, waits=[(v, rv)])
    return rv, sm


def ln_stats(ph, st, sems, eps_ap, x_ap, mean_ap, inv_ap, waits_v=(), waits_s=()):
    tok = ln_sums(ph, st, sems, x_ap, waits_v=waits_v, waits_s=waits_s)
    return ln_fin(ph, st, sems, eps_ap, tok, mean_ap, inv_ap)


def build_nc():
    nc = bass.Bass("TRN2", target_bir_lowering=False)

    xrows_h = nc.dram_tensor("xrows", [RPC, DM], F32, kind="ExternalInput").ap()
    xrep_h = nc.dram_tensor("xrep", [256, DM], F32, kind="ExternalInput").ap()
    blt_h = nc.dram_tensor("blt", [256, 256], F32, kind="ExternalInput").ap()
    clt_h = nc.dram_tensor("clt", [256, 256], F32, kind="ExternalInput").ap()
    alt_h = nc.dram_tensor("alt", [256, 256], F32, kind="ExternalInput").ap()
    rl_h = nc.dram_tensor("rl", [P, 1], F32, kind="ExternalInput").ap()
    flag_h = nc.dram_tensor("flag", [P, 1], F32, kind="ExternalInput").ap()
    ident_h = nc.dram_tensor("identb", [P, P], BF16, kind="ExternalInput").ap()
    zc_h = nc.dram_tensor("zc", [1, 1], BF16, kind="ExternalInput").ap()
    wg_h = nc.dram_tensor("wg", [DM, II], F32, kind="ExternalInput").ap()
    wu_h = nc.dram_tensor("wu", [DM, II], F32, kind="ExternalInput").ap()
    wd_h = nc.dram_tensor("wd", [II, DM], F32, kind="ExternalInput").ap()
    out_h = nc.dram_tensor("out", [RPC, DM], F32, kind="ExternalOutput").ap()

    xspill_h = nc.dram_tensor("xspill", [RPC, DM], F32, kind="Internal").ap()
    hscr_h = nc.dram_tensor("hscr", [NRT, P, II], BF16, kind="Internal").ap()
    x2t_h = nc.dram_tensor("x2t", [NRT, P, DM], BF16, kind="Internal").ap()
    ffnscr_h = nc.dram_tensor("ffnscr", [NRT, P, DM], F32, kind="Internal").ap()

    ctx0 = ExitStack()

    def sb(ctx, name, shape, dtype):
        return ctx.enter_context(nc.sbuf_tensor(name, shape, dtype))

    def psum(ctx, name, shape, dtype):
        return ctx.enter_context(nc.psum_tensor(name, shape, dtype))

    def mksem(name):
        return Sem(ctx0.enter_context(nc.semaphore(name)))

    sem_c = mksem("sem_c")
    sem_xr = mksem("sem_xr")
    sem_zc = mksem("sem_zc")
    sem_xa = [mksem(f"sem_xa{i}") for i in range(3)]
    sem_xb = [mksem(f"sem_xb{i}") for i in range(3)]
    sem_sp = [mksem(f"sem_sp{i}") for i in range(2)]
    sem_s2 = [mksem(f"sem_s2{i}") for i in range(2)]
    sem_x2r = mksem("sem_x2r")
    sem_hw = [mksem(f"sem_hw{i}") for i in range(2)]
    sem_hr = [mksem(f"sem_hr{i}") for i in range(2)]
    sem_fw = [mksem(f"sem_fw{i}") for i in range(4)]
    sem_f3 = [mksem(f"sem_f3{i}") for i in range(4)]
    sem_o = [mksem(f"sem_o{i}") for i in range(4)]
    sem_wg = [mksem(f"sem_wg{i}") for i in range(2)]
    sem_wd = [mksem(f"sem_wd{i}") for i in range(2)]
    pe = Sem(ctx0.enter_context(nc.semaphore("pe")))
    v = Sem(ctx0.enter_context(nc.semaphore("v")))
    s = Sem(ctx0.enter_context(nc.semaphore("s")))
    sems = {"pe": pe, "v": v, "s": s}

    ident_sb = sb(ctx0, "ident_sb", [P, P], BF16)
    rl_sb = sb(ctx0, "rl_sb", [P, 1], F32)
    flag_sb = sb(ctx0, "flag_sb", [P, 1], F32)
    rl1f_sb = sb(ctx0, "rl1f_sb", [P, 1], F32)
    f1m_sb = sb(ctx0, "f1m_sb", [P, 1], F32)
    eps_sb = sb(ctx0, "eps_sb", [P, 1], F32)
    st1 = sb(ctx0, "st1", [P, 2 * NRT], F32)
    strep = sb(ctx0, "strep", [P, 4], F32)
    stA = sb(ctx0, "stA", [P, 8], F32)
    lnsc = LNScratch(nc, ctx0)
    eps_ap = eps_sb[:, 0:1]

    ctx_ab = ExitStack()
    xln1t = sb(ctx_ab, "xln1t", [P, NRT * DM], BF16)
    xln1s = sb(ctx_ab, "xln1s", [P, NRT * DM], BF16)
    xrept = sb(ctx_ab, "xrept", [P, 2 * DM], BF16)
    alt_b = sb(ctx_ab, "alt_b", [P, 512], BF16)
    att_sb = sb(ctx_ab, "att_sb", [P, NSUB * DM], BF16)

    ctx_uw = ExitStack()
    U_sb = sb(ctx_uw, "U_sb", [P, 2 * DM], BF16)
    W_sb = sb(ctx_uw, "W_sb", [P, 2 * DM], BF16)

    ctx_a = ExitStack()
    NXA = 3
    xa = [sb(ctx_a, f"xa{i}", [P, DM], F32) for i in range(NXA)]
    xrep_sb = sb(ctx_a, "xrep_sb", [P, 2 * DM], F32)
    xrep_lnb = sb(ctx_a, "xrep_lnb", [P, 2 * DM], BF16)
    blt_b = sb(ctx_a, "blt_b", [P, 512], BF16)
    clt_b = sb(ctx_a, "clt_b", [P, 512], BF16)
    psu = [psum(ctx_a, f"psua{i}", [P, 512], F32) for i in range(2)]
    pst = [psum(ctx_a, f"psta{i}", [P, 1024], BF16) for i in range(2)]
    NB1 = 4
    pss = [psum(ctx_a, f"pss{i}", [P, 512], F32) for i in range(NB1)]

    ph = Ph(nc)
    d_xrep = ph.emit("sync", lambda e: e.dma_start(
        out=xrep_sb[:, :].rearrange("p (c n) -> p c n", c=2),
        in_=xrep_h.rearrange("(c p) n -> p c n", c=2)), inc=sem_xr, amt=16)
    ph.emit("sync", lambda e: e.dma_start(out=ident_sb[:, :], in_=ident_h), inc=sem_c, amt=16)
    ph.emit("sync", lambda e: e.dma_start(out=rl_sb[:, :], in_=rl_h), inc=sem_c, amt=16)
    ph.emit("sync", lambda e: e.dma_start(out=flag_sb[:, :], in_=flag_h), inc=sem_c, amt=16)
    ph.emit("gpsimd", lambda e: e.dma_start(
        out=blt_b[:, :].rearrange("p (c i) -> p c i", i=256),
        in_=blt_h.rearrange("(c p) i -> p c i", c=2)), inc=sem_c, amt=16)
    ph.emit("gpsimd", lambda e: e.dma_start(
        out=clt_b[:, :].rearrange("p (c i) -> p c i", i=256),
        in_=clt_h.rearrange("(c p) i -> p c i", c=2)), inc=sem_c, amt=16)
    ph.emit("gpsimd", lambda e: e.dma_start(
        out=alt_b[:, :].rearrange("p (c i) -> p c i", i=256),
        in_=alt_h.rearrange("(c p) i -> p c i", c=2)), inc=sem_c, amt=16)
    d_consts = sem_c.n

    ph.emit("vector", lambda e: e.memset(eps_sb[:, :], EPS))
    fv = ph.emit("vector", lambda e: e.tensor_scalar(
        f1m_sb[:, :], flag_sb[:, :], -1.0, 1.0, ALU.mult, ALU.add),
        waits=[(sem_c, d_consts)], inc=v)
    ph.emit("vector", lambda e: e.tensor_mul(rl1f_sb[:, :], rl_sb[:, :], f1m_sb[:, :]),
            waits=[(v, fv)])

    s_cast = 0

    v_replb = [None, None]
    tok_rep = [None, None]
    for jc in range(2):
        x_ap = xrep_sb[:, jc * DM : (jc + 1) * DM]
        tok_rep[jc] = ln_sums(ph, lnsc, sems, x_ap,
                              waits_v=[(sem_xr, d_xrep)], waits_s=[(sem_xr, d_xrep)])

    xa_slot = [Slot() for _ in range(NXA)]
    tok_rt = [None] * NRT
    d_xa = [None] * NRT

    def emit_load_sums(rt):
        slot = xa_slot[rt % NXA]
        d_xa[rt] = ph.emit("sync", lambda e, rt=rt: e.dma_start(
            out=xa[rt % NXA][:, :], in_=xrows_h[rt * P : (rt + 1) * P, :]),
            waits=[slot.free] if slot.free else [], inc=sem_xa[rt % NXA], amt=16)
        tok_rt[rt] = ln_sums(ph, lnsc, sems, xa[rt % NXA][:, :],
                             waits_v=[(sem_xa[rt % NXA], d_xa[rt])],
                             waits_s=[(sem_xa[rt % NXA], d_xa[rt])])

    def emit_repl_fin(jc):
        mean_ap = strep[:, 2 * jc : 2 * jc + 1]
        inv_ap = strep[:, 2 * jc + 1 : 2 * jc + 2]
        x_ap = xrep_sb[:, jc * DM : (jc + 1) * DM]
        ln_fin(ph, lnsc, sems, eps_ap, tok_rep[jc], mean_ap, inv_ap)
        v_replb[jc] = ph.emit("vector", lambda e, x_ap=x_ap, m=mean_ap, iv=inv_ap, jc=jc: e.tensor_scalar(
            xrep_lnb[:, jc * DM : (jc + 1) * DM], x_ap, m, iv, ALU.subtract, ALU.mult),
            inc=v)

    emit_load_sums(0)
    emit_repl_fin(0)
    emit_load_sums(1)
    emit_repl_fin(1)

    pst_slot = [Slot(), Slot()]
    for jc in range(2):
        for half in range(2):
            bank = pst[half % 2]
            slot = pst_slot[half % 2]

            def tgrp(e, jc=jc, half=half, bank=bank):
                last = None
                for t in range(8):
                    ns = half * 8 + t
                    last = e.transpose(
                        bank[:, t * P : (t + 1) * P],
                        xrep_lnb[:, jc * DM + ns * P : jc * DM + (ns + 1) * P],
                        ident_sb[:, :],
                    )
                return last

            g = ph.emit("tensor", tgrp,
                        waits=[(v, v_replb[jc]), (sem_c, d_consts)] + ([slot.free] if slot.free else []),
                        inc=pe)
            sc = ph.emit("scalar", lambda e, jc=jc, half=half, bank=bank: e.copy(
                xrept[:, jc * DM + half * 1024 : jc * DM + (half + 1) * 1024], bank[:, :]),
                waits=[(pe, g)], inc=s)
            slot.free = (s, sc)

    psu_slot = [Slot(), Slot()]
    s_uw = {}
    for mi, (mat_b, dst) in enumerate(((blt_b, U_sb), (clt_b, W_sb))):
        for ic in range(2):
            for nb in range(4):
                k = mi * 8 + ic * 4 + nb
                bank = psu[k % 2]
                slot = psu_slot[k % 2]

                def mmg(e, mat_b=mat_b, ic=ic, nb=nb, bank=bank):
                    last = None
                    for jc in range(2):
                        last = e.matmul(
                            bank[:, :],
                            mat_b[:, jc * 256 + ic * P : jc * 256 + (ic + 1) * P],
                            xrep_lnb[:, jc * DM + nb * 512 : jc * DM + (nb + 1) * 512],
                            start=(jc == 0), stop=(jc == 1),
                        )
                    return last

                g = ph.emit("tensor", mmg,
                            waits=[(sem_c, d_consts), (v, v_replb[1])] + ([slot.free] if slot.free else []),
                            inc=pe)
                sc = ph.emit("scalar", lambda e, dst=dst, ic=ic, nb=nb, bank=bank: e.copy(
                    dst[:, ic * DM + nb * 512 : ic * DM + (nb + 1) * 512], bank[:, :]),
                    waits=[(pe, g)], inc=s)
                s_uw[(mi, ic, nb)] = sc
                slot.free = (s, sc)

    def emit_rt_post(rt):
        mean_ap = st1[:, 2 * rt : 2 * rt + 1]
        inv_ap = st1[:, 2 * rt + 1 : 2 * rt + 2]
        ln_fin(ph, lnsc, sems, eps_ap, tok_rt[rt], mean_ap, inv_ap)
        vm = ph.emit("vector", lambda e, rt=rt, m=mean_ap, iv=inv_ap: e.tensor_scalar(
            xln1s[:, rt * DM : (rt + 1) * DM], xa[rt % NXA][:, :], m, iv,
            ALU.subtract, ALU.mult), inc=v)
        xa_slot[rt % NXA].free = (v, vm)
        for half in range(2):
            bank = pst[half % 2]
            tslot = pst_slot[half % 2]

            def tgrp(e, rt=rt, half=half, bank=bank):
                last = None
                for t in range(8):
                    ns = half * 8 + t
                    last = e.transpose(
                        bank[:, t * P : (t + 1) * P],
                        xln1s[:, rt * DM + ns * P : rt * DM + (ns + 1) * P],
                        ident_sb[:, :],
                    )
                return last

            g = ph.emit("tensor", tgrp,
                        waits=[(v, vm)] + ([tslot.free] if tslot.free else []), inc=pe)
            sc = ph.emit("scalar", lambda e, rt=rt, half=half, bank=bank: e.copy(
                xln1t[:, rt * DM + half * 1024 : rt * DM + (half + 1) * 1024], bank[:, :]),
                waits=[(pe, g)], inc=s)
            tslot.free = (s, sc)

    emit_rt_post(0)
    for rt in range(2, NRT):
        emit_load_sums(rt)
        emit_rt_post(rt - 1)
    emit_rt_post(NRT - 1)

    pss_slot = [Slot() for _ in range(NB1)]
    s_relu_last = None
    v_relu_last = None
    for mb in range(NMB):
        for nt in range(NSUB):
            k = mb * NSUB + nt
            bank = pss[k % NB1]
            slot = pss_slot[k % NB1]
            uw_wait = max(s_uw[(0, 0, nt // 4)], s_uw[(0, 1, nt // 4)],
                          s_uw[(1, 0, mb)], s_uw[(1, 1, mb)])

            def mmg(e, nt=nt, mb=mb, bank=bank):
                last = None
                for dc in range(2):
                    last = e.matmul(
                        bank[:, :],
                        U_sb[:, dc * DM + nt * P : dc * DM + (nt + 1) * P],
                        W_sb[:, dc * DM + mb * MBS : dc * DM + (mb + 1) * MBS],
                        start=(dc == 0), stop=(dc == 1),
                    )
                return last

            g = ph.emit("tensor", mmg,
                        waits=[(s, uw_wait)] + ([slot.free] if slot.free else []),
                        inc=pe)
            dst = att_sb[:, nt * DM + mb * MBS : nt * DM + (mb + 1) * MBS]
            if k % 2 == 0:
                sc = ph.emit("scalar", lambda e, dst=dst, bank=bank: e.activation(
                    dst, bank[:, :], ACT.Relu), waits=[(pe, g)], inc=s)
                slot.free = (s, sc)
                s_relu_last = sc
            else:
                vc = ph.emit("vector", lambda e, dst=dst, bank=bank: e.tensor_relu(
                    dst, bank[:, :]), waits=[(pe, g)], inc=v)
                slot.free = (v, vc)
                v_relu_last = vc
    dz = ph.emit("sync", lambda e: e.dma_start(
        out=att_sb[127:128, 15 * DM + 2047 : 15 * DM + 2048], in_=zc_h[0:1, 0:1]),
        waits=[(s, s_relu_last), (v, v_relu_last)], inc=sem_zc, amt=16)
    ph.emit("sync", lambda e: None, waits=[(sem_zc, dz)])
    ph.run()
    ctx_a.close()
    ctx_uw.close()

    ctx_b2 = ExitStack()
    NXB = 3
    xb = [sb(ctx_b2, f"xb{i}", [P, DM], F32) for i in range(NXB)]
    amid = [sb(ctx_b2, f"amid{i}", [P, DM], F32) for i in range(2)]
    xatt_b = sb(ctx_b2, "xatt_b", [P, 2 * DM], BF16)
    xln2s = [sb(ctx_b2, f"xln2s{i}", [P, DM], BF16) for i in range(2)]
    xln2s_slot = [Slot(), Slot()]
    stage2 = [sb(ctx_b2, f"stage2{i}", [P, 1024], BF16) for i in range(2)]
    st2 = sb(ctx_b2, "st2", [P, 8], F32)
    psy = [psum(ctx_b2, f"psy{i}", [P, 512], F32) for i in range(3)]
    pst2 = [psum(ctx_b2, f"pst2{i}", [P, 1024], BF16) for i in range(2)]

    ph = Ph(nc)
    psy_slot = [Slot(), Slot(), Slot()]
    pst2_slot = [Slot(), Slot()]
    xb_slot = [Slot() for _ in range(NXB)]
    psy_i = 0
    d_xb = [None] * (NRT + 4)

    def load_xb(rt):
        slot = xb_slot[rt % NXB]
        d_xb[rt] = ph.emit("sync", lambda e, rt=rt: e.dma_start(
            out=xb[rt % NXB][:, :], in_=xrows_h[rt * P : (rt + 1) * P, :]),
            waits=[slot.free] if slot.free else [], inc=sem_xb[rt % NXB], amt=16)

    for _rt in range(NXB):
        load_xb(_rt)

    v_amid = [[None] * NMB for _ in range(NRT)]
    v_amid_fin = [None] * NRT
    s_xatt = [[None] * NMB for _ in range(2)]
    stage2_slots = [Slot(), Slot()]

    def emit_y(rt):
        nonlocal psy_i
        rl_scal = rl1f_sb if rt < 2 else rl_sb
        for mb in range(NMB):
            bank = psy[psy_i % 3]
            slot = psy_slot[psy_i % 3]
            psy_i += 1

            def mmg(e, rt=rt, mb=mb, bank=bank):
                last = None
                for ns in range(NSUB):
                    last = e.matmul(
                        bank[:, :],
                        xln1t[:, rt * DM + ns * P : rt * DM + (ns + 1) * P],
                        att_sb[:, ns * DM + mb * MBS : ns * DM + (mb + 1) * MBS],
                        start=(ns == 0), stop=(ns == NSUB - 1),
                    )
                return last

            g = ph.emit("tensor", mmg, waits=[slot.free] if slot.free else [], inc=pe)
            aw = [(pe, g)]
            if rt >= 2 and tokA[rt - 2] is not None:
                aw.append((s, tokA[rt - 2][1]))
            vm = ph.emit("vector", lambda e, rt=rt, mb=mb, bank=bank, rl_scal=rl_scal: e.scalar_tensor_tensor(
                amid[rt % 2][:, mb * MBS : (mb + 1) * MBS], bank[:, :], rl_scal[:, 0:1],
                xln1s[:, rt * DM + mb * MBS : rt * DM + (mb + 1) * MBS],
                ALU.mult, ALU.add),
                waits=aw, inc=v)
            v_amid[rt][mb] = vm
            v_amid_fin[rt] = vm
            slot.free = (v, vm)

    def emit_xatt(jc):
        nonlocal psy_i
        for mb in range(NMB):
            bank = psy[psy_i % 3]
            slot = psy_slot[psy_i % 3]
            psy_i += 1

            def mmg(e, jc=jc, mb=mb, bank=bank):
                last = None
                for ns in range(NSUB):
                    last = e.matmul(
                        bank[:, :],
                        xrept[:, jc * DM + ns * P : jc * DM + (ns + 1) * P],
                        att_sb[:, ns * DM + mb * MBS : ns * DM + (mb + 1) * MBS],
                        start=(ns == 0), stop=(ns == NSUB - 1),
                    )
                return last

            g = ph.emit("tensor", mmg, waits=[slot.free] if slot.free else [], inc=pe)
            sc = ph.emit("scalar", lambda e, jc=jc, mb=mb, bank=bank: e.copy(
                xatt_b[:, jc * DM + mb * MBS : jc * DM + (mb + 1) * MBS], bank[:, :]),
                waits=[(pe, g)], inc=s)
            s_xatt[jc][mb] = sc
            slot.free = (s, sc)

    def emit_al():
        nonlocal psy_i
        for ic in range(2):
            for mb in range(NMB):
                bank = psy[psy_i % 3]
                slot = psy_slot[psy_i % 3]
                psy_i += 1

                def mmg(e, ic=ic, mb=mb, bank=bank):
                    last = None
                    for jc in range(2):
                        last = e.matmul(
                            bank[:, :],
                            alt_b[:, jc * 256 + ic * P : jc * 256 + (ic + 1) * P],
                            xatt_b[:, jc * DM + mb * MBS : jc * DM + (mb + 1) * MBS],
                            start=(jc == 0), stop=(jc == 1),
                        )
                    return last

                g = ph.emit("tensor", mmg,
                            waits=[(s, s_xatt[1][mb]), (sem_c, d_consts)] + ([slot.free] if slot.free else []),
                            inc=pe)
                vm = ph.emit("vector", lambda e, ic=ic, mb=mb, bank=bank: e.scalar_tensor_tensor(
                    amid[ic][:, mb * MBS : (mb + 1) * MBS], bank[:, :], flag_sb[:, 0:1],
                    amid[ic][:, mb * MBS : (mb + 1) * MBS], ALU.mult, ALU.add),
                    waits=[(pe, g)], inc=v)
                v_amid_fin[ic] = vm
                slot.free = (v, vm)

    tokA = [None] * NRT
    tok2 = [None] * NRT

    def emit_p1(rt):
        tokA[rt] = ln_sums(ph, lnsc, sems, amid[rt % 2][:, :],
                           waits_s=[(v, v_amid_fin[rt])])

    def emit_p2(rt):
        a_ap = amid[rt % 2][:, :]
        sA = stA[:, 2 * (rt % 4) : 2 * (rt % 4) + 1]
        sAi = stA[:, 2 * (rt % 4) + 1 : 2 * (rt % 4) + 2]
        ln_fin(ph, lnsc, sems, eps_ap, tokA[rt], sA, sAi)
        vn = ph.emit("vector", lambda e, a_ap=a_ap, sA=sA, sAi=sAi: e.tensor_scalar(
            a_ap, a_ap, sA, sAi, ALU.subtract, ALU.mult), inc=v)
        vx = ph.emit("vector", lambda e, rt=rt, a_ap=a_ap: e.tensor_add(
            xb[rt % NXB][:, :], xb[rt % NXB][:, :], a_ap),
            waits=[(v, vn), (sem_xb[rt % NXB], d_xb[rt])], inc=v)
        tok2[rt] = ln_sums(ph, lnsc, sems, xb[rt % NXB][:, :], waits_s=[(v, vx)])

    def emit_p3(rt):
        s2m = st2[:, 2 * (rt % 4) : 2 * (rt % 4) + 1]
        s2i = st2[:, 2 * (rt % 4) + 1 : 2 * (rt % 4) + 2]
        _, sq2 = ln_fin(ph, lnsc, sems, eps_ap, tok2[rt], s2m, s2i)
        sq2 = tok2[rt][1]
        xsl = xln2s_slot[rt % 2]
        vm = ph.emit("vector", lambda e, rt=rt, s2m=s2m, s2i=s2i: e.tensor_scalar(
            xln2s[rt % 2][:, :], xb[rt % NXB][:, :], s2m, s2i,
            ALU.subtract, ALU.mult),
            waits=[xsl.free] if xsl.free else [], inc=v)
        dsp = ph.emit("gpsimd", lambda e, rt=rt: e.dma_start(
            out=xspill_h[rt * P : (rt + 1) * P, :], in_=xb[rt % NXB][:, :]),
            waits=[(v, vm), (s, sq2)], inc=sem_sp[rt % 2], amt=16)
        xb_slot[rt % NXB].free = (sem_sp[rt % 2], dsp)
        if rt + NXB < NRT:
            load_xb(rt + NXB)
        for half in range(2):
            bank = pst2[half % 2]
            tslot = pst2_slot[half % 2]

            def tgrp(e, rt=rt, half=half, bank=bank):
                last = None
                for t in range(8):
                    ns = half * 8 + t
                    last = e.transpose(
                        bank[:, t * P : (t + 1) * P],
                        xln2s[rt % 2][:, ns * P : (ns + 1) * P],
                        ident_sb[:, :],
                    )
                return last

            g = ph.emit("tensor", tgrp,
                        waits=[(v, vm)] + ([tslot.free] if tslot.free else []), inc=pe)
            xsl.free = (pe, g)
            stg_slot = stage2[half % 2]
            sslot = stage2_slots[half % 2]
            sc = ph.emit("scalar", lambda e, half=half, bank=bank, stg=stg_slot: e.copy(
                stg[:, :], bank[:, :]),
                waits=[(pe, g)] + ([sslot.free] if sslot.free else []), inc=s)
            tslot.free = (s, sc)
            dd = ph.emit("gpsimd", lambda e, rt=rt, half=half, stg=stg_slot: e.dma_start(
                out=x2t_h[rt][:, half * 1024 : (half + 1) * 1024], in_=stg[:, :]),
                waits=[(s, sc)], inc=sem_s2[half % 2], amt=16)
            sslot.free = (sem_s2[half % 2], dd)

    emit_y(0)
    emit_xatt(0)
    emit_y(1)
    emit_xatt(1)
    emit_al()
    emit_p1(0)
    emit_p1(1)
    emit_p2(0)
    emit_y(2)
    emit_p1(2)
    for k in range(3, NRT):
        emit_p3(k - 3)
        emit_p2(k - 2)
        emit_y(k)
        emit_p1(k)
    emit_p2(NRT - 2)
    emit_p3(NRT - 3)
    emit_p2(NRT - 1)
    emit_p3(NRT - 2)
    emit_p3(NRT - 1)
    ph.emit("sync", lambda e: None,
            waits=[(sem_sp[0], sem_sp[0].n), (sem_sp[1], sem_sp[1].n),
                   (sem_s2[0], sem_s2[0].n), (sem_s2[1], sem_s2[1].n)])
    ph.run()
    ctx_b2.close()
    ctx_ab.close()

    ctx_f12 = ExitStack()
    wdbuf0 = sb(ctx_f12, "wdbuf0", [P, ISUBS * 512], BF16)
    ctx_f1 = ExitStack()
    xln2t = sb(ctx_f1, "xln2t", [P, NRT * DM], BF16)
    wgbuf = [sb(ctx_f1, f"wgbuf{i}", [P, NSUB * 512], BF16) for i in range(2)]
    wubuf = [sb(ctx_f1, f"wubuf{i}", [P, NSUB * 512], BF16) for i in range(2)]
    sbg = [sb(ctx_f1, f"sbg{i}", [P, ICS], F32) for i in range(2)]
    sbh = [sb(ctx_f1, f"sbh{i}", [P, ICS], F32) for i in range(2)]
    sba = [sb(ctx_f1, f"sba{i}", [P, ICS], F32) for i in range(2)]
    maskb = sb(ctx_f1, "maskb", [P, ICS], F32)
    Mb = sb(ctx_f1, "Mb", [P, ICS // 2], F32)
    mb_ = sb(ctx_f1, "mb_", [P, ICS // 2], F32)
    mmb = sb(ctx_f1, "mmb", [P, ICS // 4], F32)
    mxb = sb(ctx_f1, "mxb", [P, ICS // 4], F32)
    t2b = sb(ctx_f1, "t2b", [P, ICS // 4], F32)
    hsp = [sb(ctx_f1, f"hsp{i}", [P, ICS], BF16) for i in range(2)]
    hst = [sb(ctx_f1, f"hst{i}", [P, ICS], BF16) for i in range(2)]
    psg = [psum(ctx_f1, f"psg{i}", [P, 512], F32) for i in range(2)]
    psup = [psum(ctx_f1, f"psup{i}", [P, 512], F32) for i in range(2)]
    pstt = [psum(ctx_f1, f"pstt{i}", [P, 512], BF16) for i in range(2)]

    ph = Ph(nc)
    d_x2t_all = ph.emit("sync", lambda e: e.dma_start(
        out=xln2t[:, :].rearrange("p (r f) -> p r f", f=DM),
        in_=x2t_h.rearrange("r p f -> p r f")),
        waits=[(sem_s2[0], sem_s2[0].n), (sem_s2[1], sem_s2[1].n)],
        inc=sem_x2r, amt=16)

    NK = NIC * NRT
    snap_w = [None] * NIC
    peG = [None] * NK
    peU = [None] * NK
    peT = [None] * NK
    s_silu = [None] * NK
    s_abs = [None] * NK
    s_hst = [None] * NK
    v_h = [None] * NK
    v_hsp = [None] * NK
    d_hst = [None] * NK

    def emit_transposes(k):
        bank = pstt[k % 2]
        w = [(v, v_hsp[k])]
        if k - 2 >= 0:
            w.append((s, s_hst[k - 2]))

        def tgrp(e, k=k, bank=bank):
            last = None
            for q in range(NQ):
                last = e.transpose(
                    bank[:, q * P : (q + 1) * P],
                    hsp[k % 2][:, q * P : (q + 1) * P],
                    ident_sb[:, :],
                )
            return last

        peT[k] = ph.emit("tensor", tgrp, waits=w, inc=pe)

    def emit_hstage(k):
        wk = [(pe, peT[k])]
        if k - 2 >= 0:
            wk.append((sem_hw[k % 2], d_hst[k - 2]))
        s_hst[k] = ph.emit("scalar", lambda e, k=k: e.copy(hst[k % 2][:, :], pstt[k % 2][:, :]),
                           waits=wk, inc=s)

    def emit_hdma(k):
        ic, rt = divmod(k, NRT)
        d_hst[k] = ph.emit("sync", lambda e, ic=ic, rt=rt, k=k: e.dma_start(
            out=hscr_h[rt][:, ic * ICS : (ic + 1) * ICS], in_=hst[k % 2][:, :]),
            waits=[(s, s_hst[k])], inc=sem_hw[k % 2], amt=16)

    for ic in range(NIC):
        wslot_wait = []
        if ic - 2 >= 0:
            wslot_wait = [(pe, peU[(ic - 2) * NRT + NRT - 1])]
        ph.emit("gpsimd", lambda e, ic=ic: e.dma_start(
            out=wgbuf[ic % 2][:, :].rearrange("p (c f) -> p c f", f=512),
            in_=wg_h[:, ic * ICS : (ic + 1) * ICS].rearrange("(c p) f -> p c f", p=P)),
            waits=wslot_wait, inc=sem_wg[ic % 2], amt=16)
        snap_w[ic] = ph.emit("gpsimd", lambda e, ic=ic: e.dma_start(
            out=wubuf[ic % 2][:, :].rearrange("p (c f) -> p c f", f=512),
            in_=wu_h[:, ic * ICS : (ic + 1) * ICS].rearrange("(c p) f -> p c f", p=P)),
            inc=sem_wg[ic % 2], amt=16)

        for rt in range(NRT):
            k = ic * NRT + rt
            wg_waits = [(sem_wg[ic % 2], snap_w[ic])]
            if k == 0:
                wg_waits.append((sem_x2r, d_x2t_all))
            if k - 2 >= 0:
                wg_waits.append((s, s_silu[k - 2]))

            def gate(e, ic=ic, rt=rt, k=k):
                last = None
                for ns in range(NSUB):
                    last = e.matmul(
                        psg[k % 2][:, :],
                        xln2t[:, rt * DM + ns * P : rt * DM + (ns + 1) * P],
                        wgbuf[ic % 2][:, ns * 512 : (ns + 1) * 512],
                        start=(ns == 0), stop=(ns == NSUB - 1),
                    )
                return last

            peG[k] = ph.emit("tensor", gate, waits=wg_waits, inc=pe)

            wu_waits = []
            if k - 2 >= 0:
                wu_waits.append((v, v_h[k - 2]))

            def up(e, ic=ic, rt=rt, k=k):
                last = None
                for ns in range(NSUB):
                    last = e.matmul(
                        psup[k % 2][:, :],
                        xln2t[:, rt * DM + ns * P : rt * DM + (ns + 1) * P],
                        wubuf[ic % 2][:, ns * 512 : (ns + 1) * 512],
                        start=(ns == 0), stop=(ns == NSUB - 1),
                    )
                return last

            peU[k] = ph.emit("tensor", up, waits=wu_waits, inc=pe)
            if k - 1 >= 0:
                emit_transposes(k - 1)

            sw = [(pe, peG[k])]
            if k - 2 >= 0:
                sw.append((v, v_h[k - 2]))
            s_silu[k] = ph.emit("scalar", lambda e, k=k: e.activation(
                sbg[k % 2][:, :], psg[k % 2][:, :], ACT.Silu), waits=sw, inc=s)
            v_h[k] = ph.emit("vector", lambda e, k=k: e.tensor_mul(
                sbh[k % 2][:, :], sbg[k % 2][:, :], psup[k % 2][:, :]),
                waits=[(s, s_silu[k]), (pe, peU[k])], inc=v)
            s_abs[k] = ph.emit("scalar", lambda e, k=k: e.activation(
                sba[k % 2][:, :], sbh[k % 2][:, :], ACT.Abs), waits=[(v, v_h[k])], inc=s)
            if k - 1 >= 0:
                emit_hstage(k - 1)

            a = sba[k % 2]
            spw = [(s, s_abs[k])]
            if k - 2 >= 0:
                spw.append((pe, peT[k - 2]))
            vM = ph.emit("vector", lambda e, a=a: e.tensor_tensor(
                Mb[:, :], a[:, 0:ICS:2], a[:, 1:ICS:2], ALU.max), waits=spw, inc=v)
            vm_ = ph.emit("vector", lambda e, a=a: e.tensor_tensor(
                mb_[:, :], a[:, 0:ICS:2], a[:, 1:ICS:2], ALU.min), inc=v)
            ph.emit("vector", lambda e: e.tensor_tensor(
                mmb[:, :], Mb[:, 0 : ICS // 2 : 2], Mb[:, 1 : ICS // 2 : 2], ALU.min),
                waits=[(v, vM)])
            vmx = ph.emit("vector", lambda e: e.tensor_tensor(
                mxb[:, :], mb_[:, 0 : ICS // 2 : 2], mb_[:, 1 : ICS // 2 : 2], ALU.max),
                waits=[(v, vm_)], inc=v)
            vt2 = ph.emit("vector", lambda e: e.tensor_tensor(
                t2b[:, :], mmb[:, :], mxb[:, :], ALU.max), waits=[(v, vmx)], inc=v)
            vge = ph.emit("vector", lambda e, a=a: e.tensor_tensor(
                maskb[:, :].rearrange("p (g q) -> p g q", q=4),
                a[:, :].rearrange("p (g q) -> p g q", q=4),
                t2b[:, :].unsqueeze(2).broadcast_to([P, ICS // 4, 4]),
                ALU.is_ge), waits=[(v, vt2)], inc=v)
            v_hsp[k] = ph.emit("vector", lambda e, k=k: e.tensor_mul(
                hsp[k % 2][:, :], sbh[k % 2][:, :], maskb[:, :]),
                waits=[(v, vge)], inc=v)
            if k - 1 >= 0:
                emit_hdma(k - 1)

    k = NK - 1
    emit_transposes(k)
    emit_hstage(k)
    emit_hdma(k)
    snap_wd_pre = None
    for t in range(4):
        snap_wd_pre = ph.emit("gpsimd", lambda e, t=t: e.dma_start(
            out=wdbuf0[:, t * 16 * 512 : (t + 1) * 16 * 512].rearrange(
                "p (c f) -> p c f", f=512),
            in_=wd_h[t * 2048 : (t + 1) * 2048, 0:OBS].rearrange(
                "(c p) f -> p c f", p=P)),
            inc=sem_wd[0], amt=16)
    ph.emit("sync", lambda e: None,
            waits=[(sem_hw[0], sem_hw[0].n), (sem_hw[1], sem_hw[1].n)])
    ph.run()
    ctx_f1.close()

    ctx_f2 = ExitStack()
    wdbuf = [wdbuf0, sb(ctx_f2, "wdbuf1", [P, ISUBS * 512], BF16)]
    hrt = [sb(ctx_f2, f"hrt{i}", [P, II], BF16) for i in range(2)]
    ostage = [sb(ctx_f2, f"ostage{i}", [P, OBS], F32) for i in range(4)]
    psd = [psum(ctx_f2, f"psd{i}", [P, 512], F32) for i in range(4)]
    xt = [sb(ctx_f2, f"xt{i}", [P, DM], F32) for i in range(2)]
    ft = [sb(ctx_f2, f"ft{i}", [P, DM], F32) for i in range(2)]
    stC = sb(ctx_f2, "stC", [P, 8], F32)

    ph = Ph(nc)
    v_out = [None] * NRT
    d_out = [None] * NRT
    sq_f3 = [None] * NRT
    tokC = [None] * NRT
    d_ft = [None] * NRT

    def f3_load(rt, fw_bulk, fw_tail):
        w = [fw_bulk]
        if rt == 0:
            w += [(sem_sp[0], sem_sp[0].n), (sem_sp[1], sem_sp[1].n)]
        if rt - 2 >= 0:
            w += [(v, v_out[rt - 2]), (s, sq_f3[rt - 2]),
                  (sem_o[(rt - 2) % 4], d_out[rt - 2])]
        ph.emit("gpsimd", lambda e, rt=rt: e.dma_start(
            out=xt[rt % 2][:, :], in_=xspill_h[rt * P : (rt + 1) * P, :]),
            waits=w, inc=sem_f3[rt % 2], amt=16)
        ph.emit("gpsimd", lambda e, rt=rt: e.dma_start(
            out=ft[rt % 2][:, 0 : 3 * OBS], in_=ffnscr_h[rt][:, 0 : 3 * OBS]),
            inc=sem_f3[rt % 2], amt=16)
        d_ft[rt] = ph.emit("gpsimd", lambda e, rt=rt: e.dma_start(
            out=ft[rt % 2][:, 3 * OBS :], in_=ffnscr_h[rt][:, 3 * OBS :]),
            waits=[fw_tail], inc=sem_f3[rt % 2], amt=16)

    def f3_sums(rt):
        tokC[rt] = ln_sums(ph, lnsc, sems, ft[rt % 2][:, :],
                           waits_v=[(sem_f3[rt % 2], d_ft[rt])],
                           waits_s=[(sem_f3[rt % 2], d_ft[rt])])
        sq_f3[rt] = tokC[rt][1]

    def f3_fin(rt):
        sCm = stC[:, 2 * (rt % 4) : 2 * (rt % 4) + 1]
        sCi = stC[:, 2 * (rt % 4) + 1 : 2 * (rt % 4) + 2]
        ln_fin(ph, lnsc, sems, eps_ap, tokC[rt], sCm, sCi)
        vn3 = ph.emit("vector", lambda e, rt=rt, sCm=sCm, sCi=sCi: e.tensor_scalar(
            ft[rt % 2][:, :], ft[rt % 2][:, :], sCm, sCi,
            ALU.subtract, ALU.mult), inc=v)
        v_out[rt] = ph.emit("vector", lambda e, rt=rt: e.tensor_add(
            ft[rt % 2][:, :], ft[rt % 2][:, :], xt[rt % 2][:, :]),
            waits=[(v, vn3)], inc=v)
        d_out[rt] = ph.emit("scalar", lambda e, rt=rt: e.dma_start(
            out=out_h[rt * P : (rt + 1) * P, :], in_=ft[rt % 2][:, :]),
            waits=[(v, v_out[rt])], inc=sem_o[rt % 4], amt=16)
    NJ = NOB * NRT
    peD = [None] * NJ
    v_oev = [None] * NJ
    d_hrt = [None] * NJ
    d_ost = [None] * NJ
    snap_wd = [None] * NOB

    snap_wd[0] = snap_wd_pre
    for ob in range(NOB):
        if ob > 0:
            wsl = []
            if ob - 2 >= 0:
                wsl = [(pe, peD[(ob - 2) * NRT + NRT - 1])]
            for t in range(4):
                snap_wd[ob] = ph.emit("gpsimd", lambda e, ob=ob, t=t: e.dma_start(
                    out=wdbuf[ob % 2][:, t * 16 * 512 : (t + 1) * 16 * 512].rearrange(
                        "p (c f) -> p c f", f=512),
                    in_=wd_h[t * 2048 : (t + 1) * 2048, ob * OBS : (ob + 1) * OBS].rearrange(
                        "(c p) f -> p c f", p=P)),
                    waits=wsl if t == 0 else [], inc=sem_wd[ob % 2], amt=16)

        for rt in range(NRT):
            j = ob * NRT + rt
            hw = []
            if j - 2 >= 0:
                hw = [(pe, peD[j - 2])]
            hw0 = [(sem_hw[0], sem_hw[0].n), (sem_hw[1], sem_hw[1].n)] if j == 0 else []
            d_hrt[j] = ph.emit("sync", lambda e, rt=rt, j=j: e.dma_start(
                out=hrt[j % 2][:, :], in_=hscr_h[rt]), waits=hw + hw0,
                inc=sem_hr[j % 2], amt=16)

            mw = [(sem_wd[ob % 2], snap_wd[ob]), (sem_hr[j % 2], d_hrt[j])]
            if j - 4 >= 0:
                mw.append((v, v_oev[j - 4]))

            def down(e, ob=ob, j=j):
                last = None
                for isub in range(ISUBS):
                    last = e.matmul(
                        psd[j % 4][:, :],
                        hrt[j % 2][:, isub * P : (isub + 1) * P],
                        wdbuf[ob % 2][:, isub * 512 : (isub + 1) * 512],
                        start=(isub == 0), stop=(isub == ISUBS - 1),
                    )
                return last

            peD[j] = ph.emit("tensor", down, waits=mw, inc=pe)
            ow = [(pe, peD[j])]
            if j - 4 >= 0:
                ow.append((sem_fw[j % 4], d_ost[j - 4]))
            v_oev[j] = ph.emit("vector", lambda e, j=j: e.tensor_copy(
                ostage[j % 4][:, :], psd[j % 4][:, :]), waits=ow, inc=v)
            d_ost[j] = ph.emit("scalar", lambda e, ob=ob, rt=rt, j=j: e.dma_start(
                out=ffnscr_h[rt][:, ob * OBS : (ob + 1) * OBS], in_=ostage[j % 4][:, :]),
                waits=[(v, v_oev[j])], inc=sem_fw[j % 4], amt=16)
            if ob == NOB - 1:
                if rt >= 1:
                    f3_sums(rt - 1)
                if rt >= 2:
                    f3_fin(rt - 2)
                f3_load(rt, (sem_fw[(16 + rt) % 4], d_ost[16 + rt]),
                        (sem_fw[j % 4], d_ost[j]))

    f3_sums(NRT - 1)
    f3_fin(NRT - 2)
    f3_fin(NRT - 1)
    ph.emit("sync", lambda e: None,
            waits=[(so, so.n) for so in sem_o])
    ph.run()
    ctx_f2.close()
    ctx_f12.close()

    ctx0.close()
    return nc


def kernel(L0_attn_Z_in, L0_attn_A_l, L0_attn_r_l, L0_attn_B_l, L0_attn_C_l,
           L0_ffn_w_gate, L0_ffn_w_up, L0_ffn_w_down):
    Z = np.ascontiguousarray(np.asarray(L0_attn_Z_in, dtype=np.float32).reshape(2 * 4096, DM))
    A_l = np.asarray(L0_attn_A_l, dtype=np.float32)
    B_l = np.asarray(L0_attn_B_l, dtype=np.float32)
    C_l = np.asarray(L0_attn_C_l, dtype=np.float32)
    r_l = np.asarray(L0_attn_r_l, dtype=np.float32)
    wg = np.ascontiguousarray(np.asarray(L0_ffn_w_gate, dtype=np.float32))
    wu = np.ascontiguousarray(np.asarray(L0_ffn_w_up, dtype=np.float32))
    wd = np.ascontiguousarray(np.asarray(L0_ffn_w_down, dtype=np.float32))

    blt = np.ascontiguousarray(B_l.T)
    clt = np.ascontiguousarray(C_l.T)
    alt = np.ascontiguousarray(A_l.T)
    rl_rep = np.ascontiguousarray(np.broadcast_to(r_l.reshape(1, 1), (P, 1)).astype(np.float32))
    ident = np.eye(P, dtype=ml_dtypes.bfloat16)
    zc = np.zeros((1, 1), dtype=ml_dtypes.bfloat16)

    nc = build_nc()
    in_maps = []
    for c in range(NCORES):
        b = (c * RPC) // 4096
        in_maps.append({
            "xrows": np.ascontiguousarray(Z[c * RPC : (c + 1) * RPC]),
            "xrep": np.ascontiguousarray(Z[b * 4096 : b * 4096 + 256]),
            "blt": blt, "clt": clt, "alt": alt,
            "rl": rl_rep,
            "flag": np.full((P, 1), 1.0 if c % 4 == 0 else 0.0, dtype=np.float32),
            "identb": ident, "zc": zc,
            "wg": wg, "wu": wu, "wd": wd,
        })
    res = run_bass_kernel_spmd(nc, in_maps, core_ids=list(range(NCORES)))
    outs = [np.asarray(r["out"]) for r in res.results]
    return np.concatenate(outs, axis=0).reshape(2, 4096, DM).astype(np.float32)


# version 18
# speedup vs baseline: 1.0223x; 1.0156x over previous
import numpy as np
import ml_dtypes
from contextlib import ExitStack

import concourse.bass as bass
import concourse.mybir as mybir
from concourse.bass_utils import run_bass_kernel_spmd

F32 = mybir.dt.float32
BF16 = mybir.dt.bfloat16
ALU = mybir.AluOpType
ACT = mybir.ActivationFunctionType
AX = mybir.AxisListType

P = 128
DM = 2048
NSUB = DM // P
RPC = 1024
NRT = RPC // P
MBS = 512
NMB = DM // MBS
II = 8192
ICS = 512
NIC = II // ICS
NQ = ICS // P
OBS = 512
NOB = DM // OBS
ISUBS = II // P
NCORES = 8
EPS = 1e-5

ENGS = ("sync", "gpsimd", "scalar", "vector", "tensor")


class Sem:
    def __init__(self, h):
        self.h = h
        self.n = 0


class Slot:
    def __init__(self):
        self.free = None


class Ph:
    def __init__(self, nc):
        self.nc = nc
        self.q = {e: [] for e in ENGS}

    def emit(self, eng, fn, waits=(), inc=None, amt=1):
        ws = [(sm, val) for (sm, val) in waits if sm is not None and val > 0]
        self.q[eng].append((ws, fn, inc, amt))
        if inc is not None:
            inc.n += amt
            return inc.n
        return None

    def run(self):
        with self.nc.Block() as block:
            for e in ENGS:
                items = self.q[e]
                if not items:
                    continue

                def body(eng, items=items):
                    for ws, fn, inc, amt in items:
                        for sm, val in ws:
                            eng.wait_ge(sm.h, val)
                        inst = fn(eng)
                        if inc is not None:
                            inst.then_inc(inc.h, amt)

                getattr(block, e)(body)


class LNScratch:
    NSET = 4

    def __init__(self, nc, ctx):
        t = ctx.enter_context(nc.sbuf_tensor("ln_scr", [P, 8 * self.NSET], F32))
        self.ssum = [t[:, 8 * i + 0 : 8 * i + 1] for i in range(self.NSET)]
        self.ssq = [t[:, 8 * i + 1 : 8 * i + 2] for i in range(self.NSET)]
        self.ex2 = [t[:, 8 * i + 2 : 8 * i + 3] for i in range(self.NSET)]
        self.var = [t[:, 8 * i + 4 : 8 * i + 5] for i in range(self.NSET)]
        self.std = [t[:, 8 * i + 5 : 8 * i + 6] for i in range(self.NSET)]
        ds = [ctx.enter_context(nc.sbuf_tensor(f"ln_dump{i}", [P, DM], mybir.dt.float8e4))
              for i in range(2)]
        self.dumps = [x[:, :] for x in ds]
        self.i = 0
        self.last_ex2 = [0] * self.NSET
        self.last_mean = [0] * self.NSET
        self.last_sqrt = [0] * self.NSET


def ln_sums(ph, st, sems, x_ap, waits_v=(), waits_s=()):
    v, s = sems["v"], sems["s"]
    i = st.i % st.NSET
    dump = st.dumps[st.i % 2]
    dump_b = st.dumps[(st.i + 1) % 2]
    st.i += 1
    ph.emit(
        "scalar",
        lambda e: e.activation(dump, x_ap, ACT.Square, accum_out=st.ssq[i]),
        waits=list(waits_s) + [(v, st.last_ex2[i])],
    )
    ph.emit(
        "scalar",
        lambda e: e.activation(dump_b, x_ap, ACT.Copy, accum_out=st.ssum[i]),
        waits=list(waits_v) + [(v, st.last_mean[i])],
    )
    sm = ph.emit("scalar", lambda e: e.drain(), inc=s)
    return (i, sm)


def ln_fin(ph, st, sems, eps_ap, token, mean_ap, inv_ap):
    v, s = sems["v"], sems["s"]
    i, sm = token
    ssum, ssq, ex2, var, std = st.ssum[i], st.ssq[i], st.ex2[i], st.var[i], st.std[i]
    mn = ph.emit(
        "vector",
        lambda e: e.tensor_scalar(mean_ap, ssum, 1.0 / DM, None, ALU.mult, ALU.bypass),
        waits=[(s, sm)],
        inc=v,
    )
    st.last_mean[i] = mn
    ex = ph.emit(
        "vector",
        lambda e: e.tensor_scalar(ex2, ssq, 1.0 / DM, None, ALU.mult, ALU.bypass),
        inc=v,
    )
    st.last_ex2[i] = ex
    ph.emit("vector", lambda e: None, waits=[(v, ex)])
    nv = ph.emit(
        "vector",
        lambda e: e.scalar_tensor_tensor(var, mean_ap, mean_ap, ex2,
                                         ALU.mult, ALU.subtract),
        waits=[(s, st.last_sqrt[i])],
        inc=v,
    )
    sq2 = ph.emit(
        "scalar",
        lambda e: e.activation(std, var, ACT.Sqrt, scale=-1.0, bias=eps_ap),
        waits=[(v, nv)],
        inc=s,
    )
    st.last_sqrt[i] = sq2
    rv = ph.emit("vector", lambda e: e.reciprocal(inv_ap, std), waits=[(s, sq2)], inc=v)
    ph.emit("vector", lambda e: None, waits=[(v, rv)])
    return rv, sm


def ln_stats(ph, st, sems, eps_ap, x_ap, mean_ap, inv_ap, waits_v=(), waits_s=()):
    tok = ln_sums(ph, st, sems, x_ap, waits_v=waits_v, waits_s=waits_s)
    return ln_fin(ph, st, sems, eps_ap, tok, mean_ap, inv_ap)


def build_nc():
    nc = bass.Bass("TRN2", target_bir_lowering=False)

    xrows_h = nc.dram_tensor("xrows", [RPC, DM], F32, kind="ExternalInput").ap()
    xrep_h = nc.dram_tensor("xrep", [256, DM], F32, kind="ExternalInput").ap()
    blt_h = nc.dram_tensor("blt", [256, 256], F32, kind="ExternalInput").ap()
    clt_h = nc.dram_tensor("clt", [256, 256], F32, kind="ExternalInput").ap()
    alt_h = nc.dram_tensor("alt", [256, 256], F32, kind="ExternalInput").ap()
    rl_h = nc.dram_tensor("rl", [P, 1], F32, kind="ExternalInput").ap()
    flag_h = nc.dram_tensor("flag", [P, 1], F32, kind="ExternalInput").ap()
    ident_h = nc.dram_tensor("identb", [P, P], BF16, kind="ExternalInput").ap()
    zc_h = nc.dram_tensor("zc", [1, 1], BF16, kind="ExternalInput").ap()
    wg_h = nc.dram_tensor("wg", [DM, II], F32, kind="ExternalInput").ap()
    wu_h = nc.dram_tensor("wu", [DM, II], F32, kind="ExternalInput").ap()
    wd_h = nc.dram_tensor("wd", [II, DM], F32, kind="ExternalInput").ap()
    out_h = nc.dram_tensor("out", [RPC, DM], F32, kind="ExternalOutput").ap()

    xspill_h = nc.dram_tensor("xspill", [RPC, DM], F32, kind="Internal").ap()
    hscr_h = nc.dram_tensor("hscr", [NRT, P, II], BF16, kind="Internal").ap()
    x2t_h = nc.dram_tensor("x2t", [NRT, P, DM], BF16, kind="Internal").ap()
    ffnscr_h = nc.dram_tensor("ffnscr", [NRT, P, DM], F32, kind="Internal").ap()

    ctx0 = ExitStack()

    def sb(ctx, name, shape, dtype):
        return ctx.enter_context(nc.sbuf_tensor(name, shape, dtype))

    def psum(ctx, name, shape, dtype):
        return ctx.enter_context(nc.psum_tensor(name, shape, dtype))

    def mksem(name):
        return Sem(ctx0.enter_context(nc.semaphore(name)))

    sem_c = mksem("sem_c")
    sem_xr = mksem("sem_xr")
    sem_cg = mksem("sem_cg")
    sem_zc = mksem("sem_zc")
    sem_xa = [mksem(f"sem_xa{i}") for i in range(3)]
    sem_xb = [mksem(f"sem_xb{i}") for i in range(3)]
    sem_sp = [mksem(f"sem_sp{i}") for i in range(3)]
    sem_s2 = [mksem(f"sem_s2{i}") for i in range(2)]
    sem_x2r = mksem("sem_x2r")
    sem_hw = [mksem(f"sem_hw{i}") for i in range(2)]
    sem_hr = [mksem(f"sem_hr{i}") for i in range(2)]
    sem_fw = [mksem(f"sem_fw{i}") for i in range(4)]
    sem_f3 = [mksem(f"sem_f3{i}") for i in range(4)]
    sem_o = [mksem(f"sem_o{i}") for i in range(4)]
    sem_wg = [mksem(f"sem_wg{i}") for i in range(2)]
    sem_wd = [mksem(f"sem_wd{i}") for i in range(2)]
    pe = Sem(ctx0.enter_context(nc.semaphore("pe")))
    v = Sem(ctx0.enter_context(nc.semaphore("v")))
    s = Sem(ctx0.enter_context(nc.semaphore("s")))
    sems = {"pe": pe, "v": v, "s": s}

    ident_sb = sb(ctx0, "ident_sb", [P, P], BF16)
    rl_sb = sb(ctx0, "rl_sb", [P, 1], F32)
    flag_sb = sb(ctx0, "flag_sb", [P, 1], F32)
    rl1f_sb = sb(ctx0, "rl1f_sb", [P, 1], F32)
    f1m_sb = sb(ctx0, "f1m_sb", [P, 1], F32)
    eps_sb = sb(ctx0, "eps_sb", [P, 1], F32)
    st1 = sb(ctx0, "st1", [P, 2 * NRT], F32)
    strep = sb(ctx0, "strep", [P, 4], F32)
    stA = sb(ctx0, "stA", [P, 8], F32)
    lnsc = LNScratch(nc, ctx0)
    eps_ap = eps_sb[:, 0:1]

    ctx_ab = ExitStack()
    xln1t = sb(ctx_ab, "xln1t", [P, NRT * DM], BF16)
    xln1s = sb(ctx_ab, "xln1s", [P, NRT * DM], BF16)
    xrept = sb(ctx_ab, "xrept", [P, 2 * DM], BF16)
    alt_b = sb(ctx_ab, "alt_b", [P, 512], BF16)
    att_sb = sb(ctx_ab, "att_sb", [P, NSUB * DM], BF16)

    ctx_uw = ExitStack()
    U_sb = sb(ctx_uw, "U_sb", [P, 2 * DM], BF16)
    W_sb = sb(ctx_uw, "W_sb", [P, 2 * DM], BF16)

    ctx_a = ExitStack()
    NXA = 3
    xa = [sb(ctx_a, f"xa{i}", [P, DM], F32) for i in range(NXA)]
    xrep_sb = sb(ctx_a, "xrep_sb", [P, 2 * DM], F32)
    xrep_lnb = sb(ctx_a, "xrep_lnb", [P, 2 * DM], BF16)
    blt_b = sb(ctx_a, "blt_b", [P, 512], BF16)
    clt_b = sb(ctx_a, "clt_b", [P, 512], BF16)
    psu = [psum(ctx_a, f"psua{i}", [P, 512], F32) for i in range(2)]
    pst = [psum(ctx_a, f"psta{i}", [P, 1024], BF16) for i in range(2)]
    NB1 = 4
    pss = [psum(ctx_a, f"pss{i}", [P, 512], F32) for i in range(NB1)]

    ph = Ph(nc)
    d_xrep = ph.emit("sync", lambda e: e.dma_start(
        out=xrep_sb[:, :].rearrange("p (c n) -> p c n", c=2),
        in_=xrep_h.rearrange("(c p) n -> p c n", c=2)), inc=sem_xr, amt=16)
    ph.emit("sync", lambda e: e.dma_start(out=ident_sb[:, :], in_=ident_h), inc=sem_c, amt=16)
    ph.emit("sync", lambda e: e.dma_start(out=rl_sb[:, :], in_=rl_h), inc=sem_c, amt=16)
    ph.emit("sync", lambda e: e.dma_start(out=flag_sb[:, :], in_=flag_h), inc=sem_c, amt=16)
    ph.emit("gpsimd", lambda e: e.dma_start(
        out=blt_b[:, :].rearrange("p (c i) -> p c i", i=256),
        in_=blt_h.rearrange("(c p) i -> p c i", c=2)), inc=sem_cg, amt=16)
    ph.emit("gpsimd", lambda e: e.dma_start(
        out=clt_b[:, :].rearrange("p (c i) -> p c i", i=256),
        in_=clt_h.rearrange("(c p) i -> p c i", c=2)), inc=sem_cg, amt=16)
    ph.emit("gpsimd", lambda e: e.dma_start(
        out=alt_b[:, :].rearrange("p (c i) -> p c i", i=256),
        in_=alt_h.rearrange("(c p) i -> p c i", c=2)), inc=sem_cg, amt=16)
    d_consts = sem_c.n

    ph.emit("vector", lambda e: e.memset(eps_sb[:, :], EPS))
    fv = ph.emit("vector", lambda e: e.tensor_scalar(
        f1m_sb[:, :], flag_sb[:, :], -1.0, 1.0, ALU.mult, ALU.add),
        waits=[(sem_c, d_consts)], inc=v)
    ph.emit("vector", lambda e: e.tensor_mul(rl1f_sb[:, :], rl_sb[:, :], f1m_sb[:, :]),
            waits=[(v, fv)])

    s_cast = 0

    v_replb = [None, None]
    tok_rep = [None, None]
    for jc in range(2):
        x_ap = xrep_sb[:, jc * DM : (jc + 1) * DM]
        tok_rep[jc] = ln_sums(ph, lnsc, sems, x_ap,
                              waits_v=[(sem_xr, d_xrep)], waits_s=[(sem_xr, d_xrep)])

    xa_slot = [Slot() for _ in range(NXA)]
    tok_rt = [None] * NRT
    d_xa = [None] * NRT

    def emit_load_sums(rt):
        slot = xa_slot[rt % NXA]
        d_xa[rt] = ph.emit("sync", lambda e, rt=rt: e.dma_start(
            out=xa[rt % NXA][:, :], in_=xrows_h[rt * P : (rt + 1) * P, :]),
            waits=[slot.free] if slot.free else [], inc=sem_xa[rt % NXA], amt=16)
        tok_rt[rt] = ln_sums(ph, lnsc, sems, xa[rt % NXA][:, :],
                             waits_v=[(sem_xa[rt % NXA], d_xa[rt])],
                             waits_s=[(sem_xa[rt % NXA], d_xa[rt])])

    def emit_repl_fin(jc):
        mean_ap = strep[:, 2 * jc : 2 * jc + 1]
        inv_ap = strep[:, 2 * jc + 1 : 2 * jc + 2]
        x_ap = xrep_sb[:, jc * DM : (jc + 1) * DM]
        ln_fin(ph, lnsc, sems, eps_ap, tok_rep[jc], mean_ap, inv_ap)
        v_replb[jc] = ph.emit("vector", lambda e, x_ap=x_ap, m=mean_ap, iv=inv_ap, jc=jc: e.tensor_scalar(
            xrep_lnb[:, jc * DM : (jc + 1) * DM], x_ap, m, iv, ALU.subtract, ALU.mult),
            inc=v)

    emit_load_sums(0)
    emit_repl_fin(0)
    emit_load_sums(1)
    emit_repl_fin(1)

    pst_slot = [Slot(), Slot()]
    for jc in range(2):
        for half in range(2):
            bank = pst[half % 2]
            slot = pst_slot[half % 2]

            def tgrp(e, jc=jc, half=half, bank=bank):
                last = None
                for t in range(8):
                    ns = half * 8 + t
                    last = e.transpose(
                        bank[:, t * P : (t + 1) * P],
                        xrep_lnb[:, jc * DM + ns * P : jc * DM + (ns + 1) * P],
                        ident_sb[:, :],
                    )
                return last

            g = ph.emit("tensor", tgrp,
                        waits=[(v, v_replb[jc]), (sem_c, d_consts)] + ([slot.free] if slot.free else []),
                        inc=pe)
            sc = ph.emit("scalar", lambda e, jc=jc, half=half, bank=bank: e.copy(
                xrept[:, jc * DM + half * 1024 : jc * DM + (half + 1) * 1024], bank[:, :]),
                waits=[(pe, g)], inc=s)
            slot.free = (s, sc)

    psu_slot = [Slot(), Slot()]
    s_uw = {}
    for mi, (mat_b, dst) in enumerate(((blt_b, U_sb), (clt_b, W_sb))):
        for ic in range(2):
            for nb in range(4):
                k = mi * 8 + ic * 4 + nb
                bank = psu[k % 2]
                slot = psu_slot[k % 2]

                def mmg(e, mat_b=mat_b, ic=ic, nb=nb, bank=bank):
                    last = None
                    for jc in range(2):
                        last = e.matmul(
                            bank[:, :],
                            mat_b[:, jc * 256 + ic * P : jc * 256 + (ic + 1) * P],
                            xrep_lnb[:, jc * DM + nb * 512 : jc * DM + (nb + 1) * 512],
                            start=(jc == 0), stop=(jc == 1),
                        )
                    return last

                g = ph.emit("tensor", mmg,
                            waits=[(sem_cg, 48), (v, v_replb[1])] + ([slot.free] if slot.free else []),
                            inc=pe)
                sc = ph.emit("scalar", lambda e, dst=dst, ic=ic, nb=nb, bank=bank: e.copy(
                    dst[:, ic * DM + nb * 512 : ic * DM + (nb + 1) * 512], bank[:, :]),
                    waits=[(pe, g)], inc=s)
                s_uw[(mi, ic, nb)] = sc
                slot.free = (s, sc)

    def emit_rt_post(rt):
        mean_ap = st1[:, 2 * rt : 2 * rt + 1]
        inv_ap = st1[:, 2 * rt + 1 : 2 * rt + 2]
        ln_fin(ph, lnsc, sems, eps_ap, tok_rt[rt], mean_ap, inv_ap)
        vm = ph.emit("vector", lambda e, rt=rt, m=mean_ap, iv=inv_ap: e.tensor_scalar(
            xln1s[:, rt * DM : (rt + 1) * DM], xa[rt % NXA][:, :], m, iv,
            ALU.subtract, ALU.mult), inc=v)
        xa_slot[rt % NXA].free = (v, vm)
        for half in range(2):
            bank = pst[half % 2]
            tslot = pst_slot[half % 2]

            def tgrp(e, rt=rt, half=half, bank=bank):
                last = None
                for t in range(8):
                    ns = half * 8 + t
                    last = e.transpose(
                        bank[:, t * P : (t + 1) * P],
                        xln1s[:, rt * DM + ns * P : rt * DM + (ns + 1) * P],
                        ident_sb[:, :],
                    )
                return last

            g = ph.emit("tensor", tgrp,
                        waits=[(v, vm)] + ([tslot.free] if tslot.free else []), inc=pe)
            sc = ph.emit("scalar", lambda e, rt=rt, half=half, bank=bank: e.copy(
                xln1t[:, rt * DM + half * 1024 : rt * DM + (half + 1) * 1024], bank[:, :]),
                waits=[(pe, g)], inc=s)
            tslot.free = (s, sc)

    emit_rt_post(0)
    for rt in range(2, NRT):
        emit_load_sums(rt)
        emit_rt_post(rt - 1)
    emit_rt_post(NRT - 1)

    pss_slot = [Slot() for _ in range(NB1)]
    s_relu_last = None
    v_relu_last = None
    for mb in range(NMB):
        for nt in range(NSUB):
            k = mb * NSUB + nt
            bank = pss[k % NB1]
            slot = pss_slot[k % NB1]
            uw_wait = max(s_uw[(0, 0, nt // 4)], s_uw[(0, 1, nt // 4)],
                          s_uw[(1, 0, mb)], s_uw[(1, 1, mb)])

            def mmg(e, nt=nt, mb=mb, bank=bank):
                last = None
                for dc in range(2):
                    last = e.matmul(
                        bank[:, :],
                        U_sb[:, dc * DM + nt * P : dc * DM + (nt + 1) * P],
                        W_sb[:, dc * DM + mb * MBS : dc * DM + (mb + 1) * MBS],
                        start=(dc == 0), stop=(dc == 1),
                    )
                return last

            g = ph.emit("tensor", mmg,
                        waits=[(s, uw_wait)] + ([slot.free] if slot.free else []),
                        inc=pe)
            dst = att_sb[:, nt * DM + mb * MBS : nt * DM + (mb + 1) * MBS]
            if k % 2 == 0:
                sc = ph.emit("scalar", lambda e, dst=dst, bank=bank: e.activation(
                    dst, bank[:, :], ACT.Relu), waits=[(pe, g)], inc=s)
                slot.free = (s, sc)
                s_relu_last = sc
            else:
                vc = ph.emit("vector", lambda e, dst=dst, bank=bank: e.tensor_relu(
                    dst, bank[:, :]), waits=[(pe, g)], inc=v)
                slot.free = (v, vc)
                v_relu_last = vc
    dz = ph.emit("sync", lambda e: e.dma_start(
        out=att_sb[127:128, 15 * DM + 2047 : 15 * DM + 2048], in_=zc_h[0:1, 0:1]),
        waits=[(s, s_relu_last), (v, v_relu_last)], inc=sem_zc, amt=16)
    ph.emit("sync", lambda e: None, waits=[(sem_zc, dz)])
    ph.run()
    ctx_a.close()
    ctx_uw.close()

    ctx_b2 = ExitStack()
    NXB = 3
    xb = [sb(ctx_b2, f"xb{i}", [P, DM], F32) for i in range(NXB)]
    amid = [sb(ctx_b2, f"amid{i}", [P, DM], F32) for i in range(2)]
    xatt_b = sb(ctx_b2, "xatt_b", [P, 2 * DM], BF16)
    xln2s = [sb(ctx_b2, f"xln2s{i}", [P, DM], BF16) for i in range(2)]
    xln2s_slot = [Slot(), Slot()]
    stage2 = [sb(ctx_b2, f"stage2{i}", [P, 1024], BF16) for i in range(2)]
    st2 = sb(ctx_b2, "st2", [P, 8], F32)
    psy = [psum(ctx_b2, f"psy{i}", [P, 512], F32) for i in range(3)]
    pst2 = [psum(ctx_b2, f"pst2{i}", [P, 1024], BF16) for i in range(2)]

    ph = Ph(nc)
    psy_slot = [Slot(), Slot(), Slot()]
    pst2_slot = [Slot(), Slot()]
    xb_slot = [Slot() for _ in range(NXB)]
    psy_i = 0
    d_xb = [None] * (NRT + 4)

    def load_xb(rt):
        slot = xb_slot[rt % NXB]
        d_xb[rt] = ph.emit("sync", lambda e, rt=rt: e.dma_start(
            out=xb[rt % NXB][:, :], in_=xrows_h[rt * P : (rt + 1) * P, :]),
            waits=[slot.free] if slot.free else [], inc=sem_xb[rt % NXB], amt=16)

    for _rt in range(NXB):
        load_xb(_rt)

    v_amid = [[None] * NMB for _ in range(NRT)]
    v_amid_fin = [None] * NRT
    s_xatt = [[None] * NMB for _ in range(2)]
    stage2_slots = [Slot(), Slot()]

    def emit_y(rt):
        nonlocal psy_i
        rl_scal = rl1f_sb if rt < 2 else rl_sb
        for mb in range(NMB):
            bank = psy[psy_i % 3]
            slot = psy_slot[psy_i % 3]
            psy_i += 1

            def mmg(e, rt=rt, mb=mb, bank=bank):
                last = None
                for ns in range(NSUB):
                    last = e.matmul(
                        bank[:, :],
                        xln1t[:, rt * DM + ns * P : rt * DM + (ns + 1) * P],
                        att_sb[:, ns * DM + mb * MBS : ns * DM + (mb + 1) * MBS],
                        start=(ns == 0), stop=(ns == NSUB - 1),
                    )
                return last

            g = ph.emit("tensor", mmg, waits=[slot.free] if slot.free else [], inc=pe)
            aw = [(pe, g)]
            if rt >= 2 and tokA[rt - 2] is not None:
                aw.append((s, tokA[rt - 2][1]))
            if rt >= 2 and v_add[rt - 2] is not None:
                aw.append((v, v_add[rt - 2]))
            vm = ph.emit("vector", lambda e, rt=rt, mb=mb, bank=bank, rl_scal=rl_scal: e.scalar_tensor_tensor(
                amid[rt % 2][:, mb * MBS : (mb + 1) * MBS], bank[:, :], rl_scal[:, 0:1],
                xln1s[:, rt * DM + mb * MBS : rt * DM + (mb + 1) * MBS],
                ALU.mult, ALU.add),
                waits=aw, inc=v)
            v_amid[rt][mb] = vm
            v_amid_fin[rt] = vm
            slot.free = (v, vm)

    def emit_xatt(jc):
        nonlocal psy_i
        for mb in range(NMB):
            bank = psy[psy_i % 3]
            slot = psy_slot[psy_i % 3]
            psy_i += 1

            def mmg(e, jc=jc, mb=mb, bank=bank):
                last = None
                for ns in range(NSUB):
                    last = e.matmul(
                        bank[:, :],
                        xrept[:, jc * DM + ns * P : jc * DM + (ns + 1) * P],
                        att_sb[:, ns * DM + mb * MBS : ns * DM + (mb + 1) * MBS],
                        start=(ns == 0), stop=(ns == NSUB - 1),
                    )
                return last

            g = ph.emit("tensor", mmg, waits=[slot.free] if slot.free else [], inc=pe)
            sc = ph.emit("scalar", lambda e, jc=jc, mb=mb, bank=bank: e.copy(
                xatt_b[:, jc * DM + mb * MBS : jc * DM + (mb + 1) * MBS], bank[:, :]),
                waits=[(pe, g)], inc=s)
            s_xatt[jc][mb] = sc
            slot.free = (s, sc)

    def emit_al():
        nonlocal psy_i
        for ic in range(2):
            for mb in range(NMB):
                bank = psy[psy_i % 3]
                slot = psy_slot[psy_i % 3]
                psy_i += 1

                def mmg(e, ic=ic, mb=mb, bank=bank):
                    last = None
                    for jc in range(2):
                        last = e.matmul(
                            bank[:, :],
                            alt_b[:, jc * 256 + ic * P : jc * 256 + (ic + 1) * P],
                            xatt_b[:, jc * DM + mb * MBS : jc * DM + (mb + 1) * MBS],
                            start=(jc == 0), stop=(jc == 1),
                        )
                    return last

                g = ph.emit("tensor", mmg,
                            waits=[(s, s_xatt[1][mb]), (sem_cg, 48)] + ([slot.free] if slot.free else []),
                            inc=pe)
                vm = ph.emit("vector", lambda e, ic=ic, mb=mb, bank=bank: e.scalar_tensor_tensor(
                    amid[ic][:, mb * MBS : (mb + 1) * MBS], bank[:, :], flag_sb[:, 0:1],
                    amid[ic][:, mb * MBS : (mb + 1) * MBS], ALU.mult, ALU.add),
                    waits=[(pe, g)], inc=v)
                v_amid_fin[ic] = vm
                slot.free = (v, vm)

    tokA = [None] * NRT
    tok2 = [None] * NRT
    v_add = [None] * NRT

    def emit_p1(rt):
        tokA[rt] = ln_sums(ph, lnsc, sems, amid[rt % 2][:, :],
                           waits_s=[(v, v_amid_fin[rt])])

    def emit_p2(rt):
        a_ap = amid[rt % 2][:, :]
        sA = stA[:, 2 * (rt % 4) : 2 * (rt % 4) + 1]
        sAi = stA[:, 2 * (rt % 4) + 1 : 2 * (rt % 4) + 2]
        ln_fin(ph, lnsc, sems, eps_ap, tokA[rt], sA, sAi)
        vn = ph.emit("vector", lambda e, a_ap=a_ap, sA=sA, sAi=sAi: e.tensor_scalar(
            a_ap, a_ap, sA, sAi, ALU.subtract, ALU.mult), inc=v)
        vx = ph.emit("vector", lambda e, rt=rt, a_ap=a_ap: e.tensor_add(
            xb[rt % NXB][:, :], xb[rt % NXB][:, :], a_ap),
            waits=[(v, vn), (sem_xb[rt % NXB], d_xb[rt])], inc=v)
        v_add[rt] = vx
        tok2[rt] = ln_sums(ph, lnsc, sems, xb[rt % NXB][:, :], waits_s=[(v, vx)])

    def emit_p3(rt):
        s2m = st2[:, 2 * (rt % 4) : 2 * (rt % 4) + 1]
        s2i = st2[:, 2 * (rt % 4) + 1 : 2 * (rt % 4) + 2]
        _, sq2 = ln_fin(ph, lnsc, sems, eps_ap, tok2[rt], s2m, s2i)
        sq2 = tok2[rt][1]
        xsl = xln2s_slot[rt % 2]
        vm = ph.emit("vector", lambda e, rt=rt, s2m=s2m, s2i=s2i: e.tensor_scalar(
            xln2s[rt % 2][:, :], xb[rt % NXB][:, :], s2m, s2i,
            ALU.subtract, ALU.mult),
            waits=[xsl.free] if xsl.free else [], inc=v)
        dsp = ph.emit("gpsimd", lambda e, rt=rt: e.dma_start(
            out=xspill_h[rt * P : (rt + 1) * P, :], in_=xb[rt % NXB][:, :]),
            waits=[(v, vm), (s, sq2)], inc=sem_sp[rt % NXB], amt=16)
        xb_slot[rt % NXB].free = (sem_sp[rt % NXB], dsp)
        if rt + NXB < NRT:
            load_xb(rt + NXB)
        for half in range(2):
            bank = pst2[half % 2]
            tslot = pst2_slot[half % 2]

            def tgrp(e, rt=rt, half=half, bank=bank):
                last = None
                for t in range(8):
                    ns = half * 8 + t
                    last = e.transpose(
                        bank[:, t * P : (t + 1) * P],
                        xln2s[rt % 2][:, ns * P : (ns + 1) * P],
                        ident_sb[:, :],
                    )
                return last

            g = ph.emit("tensor", tgrp,
                        waits=[(v, vm)] + ([tslot.free] if tslot.free else []), inc=pe)
            xsl.free = (pe, g)
            stg_slot = stage2[half % 2]
            sslot = stage2_slots[half % 2]
            sc = ph.emit("scalar", lambda e, half=half, bank=bank, stg=stg_slot: e.copy(
                stg[:, :], bank[:, :]),
                waits=[(pe, g)] + ([sslot.free] if sslot.free else []), inc=s)
            tslot.free = (s, sc)
            dd = ph.emit("gpsimd", lambda e, rt=rt, half=half, stg=stg_slot: e.dma_start(
                out=x2t_h[rt][:, half * 1024 : (half + 1) * 1024], in_=stg[:, :]),
                waits=[(s, sc)], inc=sem_s2[half % 2], amt=16)
            sslot.free = (sem_s2[half % 2], dd)

    emit_y(0)
    emit_xatt(0)
    emit_y(1)
    emit_xatt(1)
    emit_al()
    emit_p1(0)
    emit_p1(1)
    emit_p2(0)
    emit_y(2)
    emit_p1(2)
    for k in range(3, NRT):
        emit_p3(k - 3)
        emit_p2(k - 2)
        emit_y(k)
        emit_p1(k)
    emit_p2(NRT - 2)
    emit_p3(NRT - 3)
    emit_p2(NRT - 1)
    emit_p3(NRT - 2)
    emit_p3(NRT - 1)
    ph.emit("sync", lambda e: None,
            waits=[(sp, sp.n) for sp in sem_sp] +
                  [(sem_s2[0], sem_s2[0].n), (sem_s2[1], sem_s2[1].n)])
    ph.run()
    ctx_b2.close()
    ctx_ab.close()

    ctx_f12 = ExitStack()
    wdbuf0 = sb(ctx_f12, "wdbuf0", [P, ISUBS * 512], BF16)
    ctx_f1 = ExitStack()
    xln2t = sb(ctx_f1, "xln2t", [P, NRT * DM], BF16)
    wgbuf = [sb(ctx_f1, f"wgbuf{i}", [P, NSUB * 512], BF16) for i in range(2)]
    wubuf = [sb(ctx_f1, f"wubuf{i}", [P, NSUB * 512], BF16) for i in range(2)]
    sbg = [sb(ctx_f1, f"sbg{i}", [P, ICS], F32) for i in range(2)]
    sbh = [sb(ctx_f1, f"sbh{i}", [P, ICS], F32) for i in range(2)]
    sba = [sb(ctx_f1, f"sba{i}", [P, ICS], F32) for i in range(2)]
    maskb = sb(ctx_f1, "maskb", [P, ICS], F32)
    Mb = sb(ctx_f1, "Mb", [P, ICS // 2], F32)
    mb_ = sb(ctx_f1, "mb_", [P, ICS // 2], F32)
    mmb = sb(ctx_f1, "mmb", [P, ICS // 4], F32)
    mxb = sb(ctx_f1, "mxb", [P, ICS // 4], F32)
    t2b = sb(ctx_f1, "t2b", [P, ICS // 4], F32)
    hsp = [sb(ctx_f1, f"hsp{i}", [P, ICS], BF16) for i in range(2)]
    hst = [sb(ctx_f1, f"hst{i}", [P, ICS], BF16) for i in range(2)]
    psg = [psum(ctx_f1, f"psg{i}", [P, 512], F32) for i in range(2)]
    psup = [psum(ctx_f1, f"psup{i}", [P, 512], F32) for i in range(2)]
    pstt = [psum(ctx_f1, f"pstt{i}", [P, 512], BF16) for i in range(2)]

    ph = Ph(nc)
    d_x2t_all = ph.emit("sync", lambda e: e.dma_start(
        out=xln2t[:, :].rearrange("p (r f) -> p r f", f=DM),
        in_=x2t_h.rearrange("r p f -> p r f")),
        waits=[(sem_s2[0], sem_s2[0].n), (sem_s2[1], sem_s2[1].n)],
        inc=sem_x2r, amt=16)

    NK = NIC * NRT
    snap_w = [None] * NIC
    peG = [None] * NK
    peU = [None] * NK
    peT = [None] * NK
    s_silu = [None] * NK
    s_abs = [None] * NK
    s_hst = [None] * NK
    v_h = [None] * NK
    v_hsp = [None] * NK
    d_hst = [None] * NK

    def emit_transposes(k):
        bank = pstt[k % 2]
        w = [(v, v_hsp[k])]
        if k - 2 >= 0:
            w.append((s, s_hst[k - 2]))

        def tgrp(e, k=k, bank=bank):
            last = None
            for q in range(NQ):
                last = e.transpose(
                    bank[:, q * P : (q + 1) * P],
                    hsp[k % 2][:, q * P : (q + 1) * P],
                    ident_sb[:, :],
                )
            return last

        peT[k] = ph.emit("tensor", tgrp, waits=w, inc=pe)

    def emit_hstage(k):
        wk = [(pe, peT[k])]
        if k - 2 >= 0:
            wk.append((sem_hw[k % 2], d_hst[k - 2]))
        s_hst[k] = ph.emit("scalar", lambda e, k=k: e.copy(hst[k % 2][:, :], pstt[k % 2][:, :]),
                           waits=wk, inc=s)

    def emit_hdma(k):
        ic, rt = divmod(k, NRT)
        d_hst[k] = ph.emit("sync", lambda e, ic=ic, rt=rt, k=k: e.dma_start(
            out=hscr_h[rt][:, ic * ICS : (ic + 1) * ICS], in_=hst[k % 2][:, :]),
            waits=[(s, s_hst[k])], inc=sem_hw[k % 2], amt=16)

    for ic in range(NIC):
        wslot_wait = []
        if ic - 2 >= 0:
            wslot_wait = [(pe, peU[(ic - 2) * NRT + NRT - 1])]
        ph.emit("gpsimd", lambda e, ic=ic: e.dma_start(
            out=wgbuf[ic % 2][:, :].rearrange("p (c f) -> p c f", f=512),
            in_=wg_h[:, ic * ICS : (ic + 1) * ICS].rearrange("(c p) f -> p c f", p=P)),
            waits=wslot_wait, inc=sem_wg[ic % 2], amt=16)
        snap_w[ic] = ph.emit("gpsimd", lambda e, ic=ic: e.dma_start(
            out=wubuf[ic % 2][:, :].rearrange("p (c f) -> p c f", f=512),
            in_=wu_h[:, ic * ICS : (ic + 1) * ICS].rearrange("(c p) f -> p c f", p=P)),
            inc=sem_wg[ic % 2], amt=16)

        for rt in range(NRT):
            k = ic * NRT + rt
            wg_waits = [(sem_wg[ic % 2], snap_w[ic])]
            if k == 0:
                wg_waits.append((sem_x2r, d_x2t_all))
            if k - 2 >= 0:
                wg_waits.append((s, s_silu[k - 2]))

            def gate(e, ic=ic, rt=rt, k=k):
                last = None
                for ns in range(NSUB):
                    last = e.matmul(
                        psg[k % 2][:, :],
                        xln2t[:, rt * DM + ns * P : rt * DM + (ns + 1) * P],
                        wgbuf[ic % 2][:, ns * 512 : (ns + 1) * 512],
                        start=(ns == 0), stop=(ns == NSUB - 1),
                    )
                return last

            peG[k] = ph.emit("tensor", gate, waits=wg_waits, inc=pe)

            wu_waits = []
            if k - 2 >= 0:
                wu_waits.append((v, v_h[k - 2]))

            def up(e, ic=ic, rt=rt, k=k):
                last = None
                for ns in range(NSUB):
                    last = e.matmul(
                        psup[k % 2][:, :],
                        xln2t[:, rt * DM + ns * P : rt * DM + (ns + 1) * P],
                        wubuf[ic % 2][:, ns * 512 : (ns + 1) * 512],
                        start=(ns == 0), stop=(ns == NSUB - 1),
                    )
                return last

            peU[k] = ph.emit("tensor", up, waits=wu_waits, inc=pe)
            if k - 1 >= 0:
                emit_transposes(k - 1)

            sw = [(pe, peG[k])]
            if k - 2 >= 0:
                sw.append((v, v_h[k - 2]))
            s_silu[k] = ph.emit("scalar", lambda e, k=k: e.activation(
                sbg[k % 2][:, :], psg[k % 2][:, :], ACT.Silu), waits=sw, inc=s)
            v_h[k] = ph.emit("vector", lambda e, k=k: e.tensor_mul(
                sbh[k % 2][:, :], sbg[k % 2][:, :], psup[k % 2][:, :]),
                waits=[(s, s_silu[k]), (pe, peU[k])], inc=v)
            s_abs[k] = ph.emit("scalar", lambda e, k=k: e.activation(
                sba[k % 2][:, :], sbh[k % 2][:, :], ACT.Abs), waits=[(v, v_h[k])], inc=s)
            if k - 1 >= 0:
                emit_hstage(k - 1)

            a = sba[k % 2]
            spw = [(s, s_abs[k])]
            if k - 2 >= 0:
                spw.append((pe, peT[k - 2]))
            vM = ph.emit("vector", lambda e, a=a: e.tensor_tensor(
                Mb[:, :], a[:, 0:ICS:2], a[:, 1:ICS:2], ALU.max), waits=spw, inc=v)
            vm_ = ph.emit("vector", lambda e, a=a: e.tensor_tensor(
                mb_[:, :], a[:, 0:ICS:2], a[:, 1:ICS:2], ALU.min), inc=v)
            ph.emit("vector", lambda e: e.tensor_tensor(
                mmb[:, :], Mb[:, 0 : ICS // 2 : 2], Mb[:, 1 : ICS // 2 : 2], ALU.min),
                waits=[(v, vM)])
            vmx = ph.emit("vector", lambda e: e.tensor_tensor(
                mxb[:, :], mb_[:, 0 : ICS // 2 : 2], mb_[:, 1 : ICS // 2 : 2], ALU.max),
                waits=[(v, vm_)], inc=v)
            vt2 = ph.emit("vector", lambda e: e.tensor_tensor(
                t2b[:, :], mmb[:, :], mxb[:, :], ALU.max), waits=[(v, vmx)], inc=v)
            vge = ph.emit("vector", lambda e, a=a: e.tensor_tensor(
                maskb[:, :].rearrange("p (g q) -> p g q", q=4),
                a[:, :].rearrange("p (g q) -> p g q", q=4),
                t2b[:, :].unsqueeze(2).broadcast_to([P, ICS // 4, 4]),
                ALU.is_ge), waits=[(v, vt2)], inc=v)
            v_hsp[k] = ph.emit("vector", lambda e, k=k: e.tensor_mul(
                hsp[k % 2][:, :], sbh[k % 2][:, :], maskb[:, :]),
                waits=[(v, vge)], inc=v)
            if k - 1 >= 0:
                emit_hdma(k - 1)

    k = NK - 1
    emit_transposes(k)
    emit_hstage(k)
    emit_hdma(k)
    snap_wd_pre = None
    for t in range(4):
        snap_wd_pre = ph.emit("gpsimd", lambda e, t=t: e.dma_start(
            out=wdbuf0[:, t * 16 * 512 : (t + 1) * 16 * 512].rearrange(
                "p (c f) -> p c f", f=512),
            in_=wd_h[t * 2048 : (t + 1) * 2048, 0:OBS].rearrange(
                "(c p) f -> p c f", p=P)),
            inc=sem_wd[0], amt=16)
    ph.emit("sync", lambda e: None,
            waits=[(sem_hw[0], sem_hw[0].n), (sem_hw[1], sem_hw[1].n)])
    ph.run()
    ctx_f1.close()

    ctx_f2 = ExitStack()
    wdbuf = [wdbuf0, sb(ctx_f2, "wdbuf1", [P, ISUBS * 512], BF16)]
    hrt = [sb(ctx_f2, f"hrt{i}", [P, II], BF16) for i in range(2)]
    ostage = [sb(ctx_f2, f"ostage{i}", [P, OBS], F32) for i in range(4)]
    psd = [psum(ctx_f2, f"psd{i}", [P, 512], F32) for i in range(4)]
    xt = [sb(ctx_f2, f"xt{i}", [P, DM], F32) for i in range(2)]
    ft = [sb(ctx_f2, f"ft{i}", [P, DM], F32) for i in range(2)]
    stC = sb(ctx_f2, "stC", [P, 8], F32)

    ph = Ph(nc)
    v_out = [None] * NRT
    d_out = [None] * NRT
    sq_f3 = [None] * NRT
    tokC = [None] * NRT
    d_ft = [None] * NRT

    def f3_load(rt, fw_bulk, fw_tail):
        w = [fw_bulk]
        if rt == 0:
            w += [(sp, sp.n) for sp in sem_sp]
        if rt - 2 >= 0:
            w += [(v, v_out[rt - 2]), (s, sq_f3[rt - 2]),
                  (sem_o[(rt - 2) % 4], d_out[rt - 2])]
        ph.emit("gpsimd", lambda e, rt=rt: e.dma_start(
            out=xt[rt % 2][:, :], in_=xspill_h[rt * P : (rt + 1) * P, :]),
            waits=w, inc=sem_f3[rt % 2], amt=16)
        ph.emit("gpsimd", lambda e, rt=rt: e.dma_start(
            out=ft[rt % 2][:, 0 : 3 * OBS], in_=ffnscr_h[rt][:, 0 : 3 * OBS]),
            inc=sem_f3[rt % 2], amt=16)
        d_ft[rt] = ph.emit("gpsimd", lambda e, rt=rt: e.dma_start(
            out=ft[rt % 2][:, 3 * OBS :], in_=ffnscr_h[rt][:, 3 * OBS :]),
            waits=[fw_tail], inc=sem_f3[rt % 2], amt=16)

    def f3_sums(rt):
        tokC[rt] = ln_sums(ph, lnsc, sems, ft[rt % 2][:, :],
                           waits_v=[(sem_f3[rt % 2], d_ft[rt])],
                           waits_s=[(sem_f3[rt % 2], d_ft[rt])])
        sq_f3[rt] = tokC[rt][1]

    def f3_fin(rt):
        sCm = stC[:, 2 * (rt % 4) : 2 * (rt % 4) + 1]
        sCi = stC[:, 2 * (rt % 4) + 1 : 2 * (rt % 4) + 2]
        ln_fin(ph, lnsc, sems, eps_ap, tokC[rt], sCm, sCi)
        vn3 = ph.emit("vector", lambda e, rt=rt, sCm=sCm, sCi=sCi: e.tensor_scalar(
            ft[rt % 2][:, :], ft[rt % 2][:, :], sCm, sCi,
            ALU.subtract, ALU.mult), inc=v)
        v_out[rt] = ph.emit("vector", lambda e, rt=rt: e.tensor_add(
            ft[rt % 2][:, :], ft[rt % 2][:, :], xt[rt % 2][:, :]),
            waits=[(v, vn3)], inc=v)
        d_out[rt] = ph.emit("scalar", lambda e, rt=rt: e.dma_start(
            out=out_h[rt * P : (rt + 1) * P, :], in_=ft[rt % 2][:, :]),
            waits=[(v, v_out[rt])], inc=sem_o[rt % 4], amt=16)
    NJ = NOB * NRT
    peD = [None] * NJ
    v_oev = [None] * NJ
    d_hrt = [None] * NJ
    d_ost = [None] * NJ
    snap_wd = [None] * NOB

    snap_wd[0] = snap_wd_pre
    for ob in range(NOB):
        if ob > 0:
            wsl = []
            if ob - 2 >= 0:
                wsl = [(pe, peD[(ob - 2) * NRT + NRT - 1])]
            for t in range(4):
                snap_wd[ob] = ph.emit("gpsimd", lambda e, ob=ob, t=t: e.dma_start(
                    out=wdbuf[ob % 2][:, t * 16 * 512 : (t + 1) * 16 * 512].rearrange(
                        "p (c f) -> p c f", f=512),
                    in_=wd_h[t * 2048 : (t + 1) * 2048, ob * OBS : (ob + 1) * OBS].rearrange(
                        "(c p) f -> p c f", p=P)),
                    waits=wsl if t == 0 else [], inc=sem_wd[ob % 2], amt=16)

        for rt in range(NRT):
            j = ob * NRT + rt
            hw = []
            if j - 2 >= 0:
                hw = [(pe, peD[j - 2])]
            hw0 = [(sem_hw[0], sem_hw[0].n), (sem_hw[1], sem_hw[1].n)] if j == 0 else []
            d_hrt[j] = ph.emit("sync", lambda e, rt=rt, j=j: e.dma_start(
                out=hrt[j % 2][:, :], in_=hscr_h[rt]), waits=hw + hw0,
                inc=sem_hr[j % 2], amt=16)

            mw = [(sem_wd[ob % 2], snap_wd[ob]), (sem_hr[j % 2], d_hrt[j])]
            if j - 4 >= 0:
                mw.append((v, v_oev[j - 4]))

            def down(e, ob=ob, j=j):
                last = None
                for isub in range(ISUBS):
                    last = e.matmul(
                        psd[j % 4][:, :],
                        hrt[j % 2][:, isub * P : (isub + 1) * P],
                        wdbuf[ob % 2][:, isub * 512 : (isub + 1) * 512],
                        start=(isub == 0), stop=(isub == ISUBS - 1),
                    )
                return last

            peD[j] = ph.emit("tensor", down, waits=mw, inc=pe)
            ow = [(pe, peD[j])]
            if j - 4 >= 0:
                ow.append((sem_fw[j % 4], d_ost[j - 4]))
            v_oev[j] = ph.emit("vector", lambda e, j=j: e.tensor_copy(
                ostage[j % 4][:, :], psd[j % 4][:, :]), waits=ow, inc=v)
            d_ost[j] = ph.emit("scalar", lambda e, ob=ob, rt=rt, j=j: e.dma_start(
                out=ffnscr_h[rt][:, ob * OBS : (ob + 1) * OBS], in_=ostage[j % 4][:, :]),
                waits=[(v, v_oev[j])], inc=sem_fw[j % 4], amt=16)
            if ob == NOB - 1:
                if rt >= 1:
                    f3_sums(rt - 1)
                if rt >= 2:
                    f3_fin(rt - 2)
                f3_load(rt, (sem_fw[(16 + rt) % 4], d_ost[16 + rt]),
                        (sem_fw[j % 4], d_ost[j]))

    f3_sums(NRT - 1)
    f3_fin(NRT - 2)
    f3_fin(NRT - 1)
    ph.emit("sync", lambda e: None,
            waits=[(so, so.n) for so in sem_o])
    ph.run()
    ctx_f2.close()
    ctx_f12.close()

    ctx0.close()
    return nc


def kernel(L0_attn_Z_in, L0_attn_A_l, L0_attn_r_l, L0_attn_B_l, L0_attn_C_l,
           L0_ffn_w_gate, L0_ffn_w_up, L0_ffn_w_down):
    Z = np.ascontiguousarray(np.asarray(L0_attn_Z_in, dtype=np.float32).reshape(2 * 4096, DM))
    A_l = np.asarray(L0_attn_A_l, dtype=np.float32)
    B_l = np.asarray(L0_attn_B_l, dtype=np.float32)
    C_l = np.asarray(L0_attn_C_l, dtype=np.float32)
    r_l = np.asarray(L0_attn_r_l, dtype=np.float32)
    wg = np.ascontiguousarray(np.asarray(L0_ffn_w_gate, dtype=np.float32))
    wu = np.ascontiguousarray(np.asarray(L0_ffn_w_up, dtype=np.float32))
    wd = np.ascontiguousarray(np.asarray(L0_ffn_w_down, dtype=np.float32))

    blt = np.ascontiguousarray(B_l.T)
    clt = np.ascontiguousarray(C_l.T)
    alt = np.ascontiguousarray(A_l.T)
    rl_rep = np.ascontiguousarray(np.broadcast_to(r_l.reshape(1, 1), (P, 1)).astype(np.float32))
    ident = np.eye(P, dtype=ml_dtypes.bfloat16)
    zc = np.zeros((1, 1), dtype=ml_dtypes.bfloat16)

    nc = build_nc()
    in_maps = []
    for c in range(NCORES):
        b = (c * RPC) // 4096
        in_maps.append({
            "xrows": np.ascontiguousarray(Z[c * RPC : (c + 1) * RPC]),
            "xrep": np.ascontiguousarray(Z[b * 4096 : b * 4096 + 256]),
            "blt": blt, "clt": clt, "alt": alt,
            "rl": rl_rep,
            "flag": np.full((P, 1), 1.0 if c % 4 == 0 else 0.0, dtype=np.float32),
            "identb": ident, "zc": zc,
            "wg": wg, "wu": wu, "wd": wd,
        })
    res = run_bass_kernel_spmd(nc, in_maps, core_ids=list(range(NCORES)))
    outs = [np.asarray(r["out"]) for r in res.results]
    return np.concatenate(outs, axis=0).reshape(2, 4096, DM).astype(np.float32)
